# version 24
# baseline (speedup 1.0000x reference)
import numpy as np
from contextlib import ExitStack
import concourse.bass as bass
import concourse.mybir as mybir
from concourse.bass_utils import run_bass_kernel_spmd
import ml_dtypes

F32 = mybir.dt.float32
I32 = mybir.dt.int32
BF16 = mybir.dt.bfloat16
AF = mybir.ActivationFunctionType
ALU = mybir.AluOpType
AX = mybir.AxisListType

P = 128
D = 1024
FI = 2816
H = 16
NPAIR = H // 2
KD = D // 128
KI = FI // 128
VSTR = 80

ENGINES = ("sync", "vector", "scalar", "tensor", "gpsimd")
BAR_ENGINES = ("sync", "vector", "scalar", "tensor", "gpsimd")


class Sched:
    def __init__(self, nc, es):
        self.nc = nc
        self.es = es
        self.q = {e: [] for e in ENGINES}
        self.sems = {}
        self.counts = {}
        self.flush = {}

    def sem(self, name):
        self.sems[name] = self.es.enter_context(self.nc.semaphore(name))
        self.counts[name] = 0
        return name

    def val(self, name):
        return self.counts[name]

    def on(self, eng, fn, wait=(), inc=None, amt=1):
        after = None
        wait = [w for w in wait if w is not None and w[1] is not None]
        if inc is not None:
            if isinstance(inc, tuple):
                inc, amt = inc
            before = self.counts[inc]
            self.counts[inc] += amt
            after = self.counts[inc]
            if amt == 16 and after - self.flush.get(inc, 0) > 128:
                wait.append((inc, before))
                self.flush[inc] = before
        self.q[eng].append((fn, tuple(wait), inc, amt))
        return after

    def barrier(self, dma_sems=()):
        for e in BAR_ENGINES:
            if e == "sync":
                w = [(nm, self.counts[nm]) for nm in dma_sems]
                self.on(e, lambda en: en.nop(), wait=w, inc="ph")
            else:
                self.on(e, lambda en: (en.drain(), en.nop())[1], inc="ph")
        return self.val("ph")

    def play(self, block):
        def run(e, name):
            for fn, waits, inc, amt in self.q[name]:
                for (sname, v) in waits:
                    e.wait_ge(self.sems[sname], v)
                r = fn(e)
                if inc is not None:
                    assert r is not None, f"{name} op inc={inc} returned None"
                    r.then_inc(self.sems[inc], amt)
        for name in ENGINES:
            if not self.q[name]:
                continue
            getattr(block, name)(lambda e, n=name: run(e, n))


class QPipe:

    def __init__(self, s, t, t2):
        self.s = s
        self.sets = [t, t2]
        self.sh = t
        self.pend = None
        self.idx = 0

    @property
    def pending_tag(self):
        return self.pend[4] if self.pend else None

    def push(self, x_ap, xq_out, C, vtok=None, clamp30=False, fin_wait=None,
             tag=None):
        s = self.s
        p = self.idx & 1
        self.idx += 1
        ts = self.sets[p]
        b1, bmax, hs, rs = ts["b1"], ts["bmax"], ts["hs"], ts["rs"]
        b2 = self.sh["b2"]
        B = C // 16
        x3 = x_ap.rearrange("p (b k) -> p b k", k=16)
        clamp = 1e-30 if clamp30 else 1e-12

        def v_head(v, x3=x3, b1=b1, bmax=bmax, hs=hs, rs=rs, B=B, C=C,
                   clamp=clamp):
            v.drain()
            v.reduce_max(bmax[:, :B], x3, axis=AX.X, apply_absolute_value=True)
            v.drain()
            v.tensor_scalar(bmax[:, :B], bmax[:, :B], 1.0 / 6.0, clamp,
                            ALU.mult, ALU.max)
            v.drain()
            v.tensor_scalar(hs[:, :B], bmax[:, :B], 0.5, None, ALU.mult)
            v.reciprocal(rs[:, :B], bmax[:, :B])
            v.drain()
            rsb = rs[:, :B].unsqueeze(2).broadcast_to([128, B, 16])
            return v.scalar_tensor_tensor(
                b1[:, :C].rearrange("p (b k) -> p b k", k=16), x3, 1.0, rsb,
                ALU.mult, ALU.mult)

        xv = s.on("vector", v_head, wait=(vtok or ()), inc="vq")
        sqv = s.on("scalar", lambda e, b2=b2, b1=b1, C=C:
                   e.activation(b2[:, :C], b1[:, :C], AF.Square),
                   wait=[("vq", xv + 1 if self.pend else xv)], inc="sq")
        tok = self._retire(fin_wait) if self.pend else None
        self.pend = (p, xq_out, C, sqv, tag)
        return tok

    def _retire(self, fin_wait=None):
        s = self.s
        p, xq_out, C, sqv, tag = self.pend
        self.pend = None
        ts = self.sets[p]
        b1, hs = ts["b1"], ts["hs"]
        sh = self.sh
        b2, b3, b4, b5, ri = sh["b2"], sh["b3"], sh["b4"], sh["b5"], sh["ri"]
        B = C // 16

        def v_mid(v, b1=b1, C=C):
            v.drain()
            v.tensor_scalar(b3[:, :C], b2[:, :C], 4.0, -1.0, ALU.is_ge, ALU.mult)
            v.tensor_scalar(b4[:, :C], b2[:, :C], 16.0, -0.5, ALU.is_ge, ALU.mult)
            v.drain()
            v.scalar_tensor_tensor(b5[:, :C], b3[:, :C], 2.0, b4[:, :C],
                                   ALU.add, ALU.add)
            v.drain()
            return v.scalar_tensor_tensor(b1[:, :C], b1[:, :C], 1.0, b5[:, :C],
                                          ALU.mult, ALU.mult)

        uv = s.on("vector", v_mid, wait=[("sq", sqv)], inc="vq")
        rv = s.on("scalar", lambda e, b1=b1, C=C:
                  e.activation(ri[:, :C], b1[:, :C], AF.Copy,
                               scale=32768.0, bias=64.5 * 32768.0),
                  wait=[("vq", uv)], inc="sq")

        def v_tail(v, hs=hs, C=C, B=B):
            v.scalar_tensor_tensor(b4[:, :C], b4[:, :C], -4.0, b3[:, :C],
                                   ALU.mult, ALU.subtract)
            v.drain()
            hsb = hs[:, :B].unsqueeze(2).broadcast_to([128, B, 16])
            v.scalar_tensor_tensor(b3[:, :C].rearrange("p (b k) -> p b k", k=16),
                                   b4[:, :C].rearrange("p (b k) -> p b k", k=16),
                                   1.0, hsb, ALU.add, ALU.mult)

        s.on("vector", v_tail)

        def v_final(v, xq_out=xq_out, C=C):
            v.tensor_scalar(ri[:, :C], ri[:, :C], 15, None,
                            ALU.arith_shift_right)
            v.drain()
            return v.scalar_tensor_tensor(xq_out[:, :C], ri[:, :C], -64.0,
                                          b3[:, :C], ALU.add, ALU.mult)

        qv = s.on("vector", v_final, wait=[("sq", rv)] + list(fin_wait or []),
                  inc="vq")
        return (tag, qv) if tag is not None else qv

    def flush(self, fin_wait=None):
        return self._retire(fin_wait) if self.pend else None


class SAlloc:

    BASE = 24576
    LIMIT = 224 * 1024 - 64

    def __init__(self, nc):
        self.nc = nc
        self.perm_top = self.BASE
        self.phase_base = None
        self.cur = None
        self.uid = 0

    def _alloc(self, name, shape, dt, off):
        nbytes = int(np.prod(shape[1:])) * {F32: 4, I32: 4, BF16: 2}[dt]
        nbytes = (nbytes + 31) & ~31
        self.uid += 1
        h = self.nc.alloc_sbuf_tensor_at("%s_u%d" % (name, self.uid), shape, dt,
                                         offset=off)
        return h.ap(), off + nbytes

    def perm(self, name, shape, dt=F32):
        assert self.phase_base is None, "perm alloc after phase start"
        ap_, self.perm_top = self._alloc(name, shape, dt, self.perm_top)
        return ap_

    def perm_alias(self, name, shape, dt, off):
        ap_, _ = self._alloc(name, shape, dt, off)
        return ap_

    def start_phases(self):
        self.phase_base = (self.perm_top + 63) & ~63

    def region(self):
        self.cur = self.phase_base
        return self

    def ph(self, name, shape, dt=F32):
        ap_, self.cur = self._alloc(name, shape, dt, self.cur)
        assert self.cur <= self.LIMIT, f"sbuf phase overflow {self.cur}"
        return ap_

    def sub(self):
        return self.cur

    def set(self, off):
        self.cur = off


def build_program(T=4096, CH=512):
    NT = T // 128
    NCH = T // CH
    CHT = CH // 128
    MT = T // 128
    GUC = []
    c = 0
    while c < FI:
        w = min(512, FI - c)
        GUC.append((c, w))
        c += w
    NGC = len(GUC)

    nc = bass.Bass("TRN2", target_bir_lowering=False, debug=False)
    x_in = nc.dram_tensor("x", [T, D], F32, kind="ExternalInput").ap()
    wg_in = nc.dram_tensor("w_gate", [FI, D], F32, kind="ExternalInput").ap()
    wu_in = nc.dram_tensor("w_up", [FI, D], F32, kind="ExternalInput").ap()
    wd_in = nc.dram_tensor("w_down", [D, FI], F32, kind="ExternalInput").ap()
    wq_in = nc.dram_tensor("q_proj", [H * 64, D], F32, kind="ExternalInput").ap()
    wk_in = nc.dram_tensor("k_proj", [D, 64], F32, kind="ExternalInput").ap()
    wv_in = nc.dram_tensor("v_proj", [D, 64], F32, kind="ExternalInput").ap()
    wo_in = nc.dram_tensor("o_proj", [D, H * 64], F32, kind="ExternalInput").ap()
    ident_in = nc.dram_tensor("identc", [P, P], BF16, kind="ExternalInput").ap()
    onesblk_in = nc.dram_tensor("onesblkc", [P, 2], F32, kind="ExternalInput").ap()
    ones2_in = nc.dram_tensor("ones2c", [2, P], F32, kind="ExternalInput").ap()
    onesc_in = nc.dram_tensor("onescc", [1, P], F32, kind="ExternalInput").ap()
    out_hbm = nc.dram_tensor("out", [T, D], F32, kind="ExternalOutput").ap()
    x1_scr = nc.dram_tensor("x1scr", [T, D], F32, kind="Internal").ap()
    x2_scr = nc.dram_tensor("x2scr", [T, D], F32, kind="Internal").ap()
    hq_scr = nc.dram_tensor("hqscr", [T, FI], BF16, kind="Internal").ap()
    q_scr = nc.dram_tensor("qscr", [NPAIR, P, T], BF16, kind="Internal").ap()
    wgT_scr = nc.dram_tensor("wgTscr", [P, KD * FI], BF16, kind="Internal").ap()
    wuT_scr = nc.dram_tensor("wuTscr", [P, KD * FI], BF16, kind="Internal").ap()
    wdT_scr = nc.dram_tensor("wdTscr", [P, KI * D], BF16, kind="Internal").ap()
    rs_scr = nc.dram_tensor("rsscr", [2, CH], F32, kind="Internal").ap()

    es = ExitStack()
    with es:
        A = SAlloc(nc)
        ident = A.perm("ident", [P, P], BF16)
        onesblk = A.perm("onesblk", [P, 2])
        ones2 = A.perm("ones2", [2, P])
        onesc = A.perm("onesc", [1, P])
        half = A.perm("half", [P, 1])
        zb = A.perm("zb", [P, 1])
        nhalf = A.perm("nhalf", [P, 1])
        qmaxb = A.perm("qmaxb", [P, 1])
        cbuf = A.perm("cbuf", [P, 16])
        negc = A.perm("negc", [P, 1])
        mvb = [A.perm("mv%d" % i, [P, 2]) for i in range(2)]
        vpeb = [A.perm("vpe%d" % i, [P, 1]) for i in range(2)]
        rstdb = [A.perm("rstd%d" % i, [P, 1]) for i in range(2)]
        negmeanb = [A.perm("negmean%d" % i, [P, 1]) for i in range(2)]
        nmrb = [A.perm("nmr%d" % i, [P, 1]) for i in range(2)]
        x0b = [A.perm("x0b%d" % i, [P, D]) for i in range(2)]
        xcb = A.perm("xcb", [P, D])
        xqb = A.perm("xqb", [P, D], BF16)
        xqTb = [A.perm("xqTb%d" % i, [P, KD * 128], BF16) for i in range(2)]
        sgb = A.perm("sgb", [P, 512])
        hb = A.perm("hb", [P, 512])
        hqb = A.perm("hqb", [P, FI], BF16)
        hqTb = A.perm("hqTb", [P, KI * 128], BF16)
        x1b = [A.perm("x1b%d" % i, [P, D]) for i in range(2)]
        tmp = {k: A.perm("t_" + k, [P, 512]) for k in
               ("b1", "b2", "b3", "b4")}
        b5_off = A.perm_top
        tmp["b5"] = A.perm("t_b5", [P, 512])
        tmp["ri"] = A.perm_alias("t_ri", [P, 512], I32, b5_off)
        for k in ("bmax", "hs", "rs"):
            tmp[k] = A.perm("t_" + k, [P, 32])
        tmp2 = {"b1": A.perm("t2_b1", [P, 512])}
        for k in ("bmax", "hs", "rs"):
            tmp2[k] = A.perm("t2_" + k, [P, 32])
        lnb = A.perm("lnb", [P, 12])
        ln2bb = A.perm("ln2bb", [P, D], BF16)
        ln2Tb = A.perm("ln2Tb", [P, KD * 128], BF16)
        qpb = A.perm("qpb", [P, 128], BF16)
        vtb = A.perm("vtb", [P, 128], BF16)
        q2b = A.perm("q2b", [P, NPAIR * 128])
        k2b = A.perm("k2b", [P, 512])
        A.start_phases()

        A.region()
        wgT_a = A.ph("wgT_a", [P, KD * FI], BF16)
        wuT_a = A.ph("wuT_a", [P, KD * FI], BF16)
        wst = [A.ph("wst%d" % i, [P, D]) for i in range(2)]
        wqt = A.ph("wqt", [P, D], BF16)

        A.region()
        KTd = A.ph("KTd", [P, T], BF16)
        Vp = A.ph("Vp", [P, MT * VSTR], BF16)
        woT = A.ph("woT", [P, NPAIR * D], BF16)
        subB = A.sub()
        wqT = A.ph("wqT", [P, NPAIR * KD * 128], BF16)
        wkv = A.ph("wkv", [P, KD * 128], BF16)
        wdT_b = A.ph("wdT_b", [P, KI * D], BF16)
        wdst = [A.ph("wdst%d" % i, [P, FI]) for i in range(2)]
        qpba = A.ph("qpba", [P, NPAIR * 128], BF16)
        A.set(subB)
        x1cb = [A.ph("x1cb%d" % i, [P, CHT * D]) for i in range(2)]
        qchb = A.ph("qchb", [P, NPAIR * CH], BF16)
        attb = [A.ph("attb%d" % i, [P, 2 * CH], BF16) for i in range(4)]
        ostk = A.ph("ostk", [P, NPAIR * CH], BF16)
        x2cb = [A.ph("x2cb%d" % i, [P, CHT * D]) for i in range(2)]
        oBb = A.ph("oBb", [P, CH], BF16)
        sumsb = A.ph("sumsb", [P, 2 * CH])
        rsum2 = A.ph("rsum2", [P, CH])
        rsbb = A.ph("rsbb", [P, CH])

        A.region()
        wgT_c = A.ph("wgT_c", [P, KD * FI], BF16)
        wuT_c = A.ph("wuT_c", [P, KD * FI], BF16)
        wdT_c = A.ph("wdT_c", [P, KI * D], BF16)

        PBW = [es.enter_context(nc.psum_tensor("pbw%d" % i, [P, 1024], F32))
               for i in range(4)]
        pbwf = [b.ap() for b in PBW]
        pbf = [PBW[i // 2].ap()[:, (i % 2) * 512:(i % 2) * 512 + 512]
               for i in range(8)]
        pbh = [PBW[i // 2].ap().bitcast(BF16)[:, (i % 2) * 1024:(i % 2) * 1024 + 1024]
               for i in range(8)]

        s = Sched(nc, es)
        for nm in ("ldw0", "ldw1", "ldc", "ldx0", "ldx1", "ldh", "ldq",
                   "vq", "sq", "tq", "gq", "swh", "swx0", "swx1", "sww",
                   "swc", "st", "stq", "std", "sto", "so0", "so1", "ph"):
            s.sem(nm)
        DMAS = ("ldw0", "ldw1", "ldc", "ldx0", "ldx1", "ldh", "ldq", "swh",
                "swx0", "swx1", "sww", "swc", "st", "stq", "std", "sto",
                "so0", "so1")
        block = es.enter_context(nc.Block())
        qp = QPipe(s, tmp, tmp2)

        s.on("sync", lambda e: e.dma_start(out=ident[:], in_=ident_in), inc=("ldc", 16))
        s.on("sync", lambda e: e.dma_start(out=onesblk[:, 0:2], in_=onesblk_in), inc=("ldc", 16))
        s.on("sync", lambda e: e.dma_start(out=ones2[0:2, 0:128], in_=ones2_in), inc=("ldc", 16))
        t_cst = s.on("sync", lambda e: e.dma_start(out=onesc[0:1, 0:128], in_=onesc_in), inc=("ldc", 16))

        def g_init(g):
            g.memset(half[0:1, 0:1], 0.5)
            g.memset(zb[:, 0:1], 0.0)
            g.memset(nhalf[:, 0:1], -0.5)
            g.memset(qmaxb[0:2, 0:1], 0.0)
            return g.memset(cbuf[0:2, 2:3], 0.0)
        t_ginit = s.on("gpsimd", g_init, inc="gq")

        trst = {"n": 0, "ev": {}}

        def emit_tr(src_ap, dst_ap, wait_tok=None):
            b = 6 + (trst["n"] % 2)
            wdt = src_ap.shape[0]
            w = list(wait_tok) if wait_tok else []
            if trst["n"] == 0:
                w.append(("ldc", t_cst))
            if trst["n"] >= 2:
                w.append(("sq", trst["ev"][trst["n"] - 2]))
            tt = s.on("tensor", lambda te, bb=b, sp=src_ap, ww=wdt: te.transpose(
                pbh[bb][:, 0:ww], sp, ident[0:ww, 0:ww]), wait=w, inc="tq")
            ev = s.on("scalar", lambda e, bb=b, dd=dst_ap, ww=wdt: e.copy(
                dd, pbh[bb][:, 0:ww]), wait=[("tq", tt)], inc="sq")
            trst["ev"][trst["n"]] = ev
            trst["n"] += 1
            return ev

        def emit_trb(srcs, dst_ap, wait_tok=None):
            n = len(srcs)
            b = 6 + (trst["n"] % 2)
            w = list(wait_tok) if wait_tok else []
            if trst["n"] == 0:
                w.append(("ldc", t_cst))
            if trst["n"] >= 2:
                w.append(("sq", trst["ev"][trst["n"] - 2]))

            def f(te, _srcs=srcs, _b=b):
                last = None
                for k, sp in enumerate(_srcs):
                    last = te.transpose(pbh[_b][:, k * 128:(k + 1) * 128], sp,
                                        ident[:])
                return last
            tt = s.on("tensor", f, wait=w, inc="tq")
            ev = s.on("scalar", lambda e, _b=b, dd=dst_ap, _n=n: e.copy(
                dd, pbh[_b][:, 0:_n * 128].rearrange("p (k m) -> p k m", m=128)),
                wait=[("tq", tt)], inc="sq")
            trst["ev"][trst["n"]] = ev
            trst["n"] += 1
            return ev

        def prep_ffn_w(w_hbm, stage, nrows_tiles, dst_of, prev_q, init_ev):
            last_ev = [init_ev]
            qtoks = {}
            prev_id = [None]

            def on_tok(pid, tok):
                tt_, ch_ = pid
                if ch_ == 1:
                    qtoks[tt_] = tok
                    last_ev[0] = emit_trb(
                        [wqt[:, k * 128:(k + 1) * 128] for k in range(KD)],
                        dst_of(tt_), [("vq", tok)])

            for tt in range(nrows_tiles):
                buf = stage[tt % 2]
                lw = []
                if tt - 2 in qtoks:
                    lw.append(("vq", qtoks[tt - 2]))
                elif tt < 2 and prev_q[0] is not None:
                    lw.append(("vq", prev_q[0]))
                t_l = s.on("sync", lambda e, b=buf, r=tt, _w=w_hbm:
                           e.dma_start(out=b[:, 0:D],
                                       in_=_w[r * 128:(r + 1) * 128, :]),
                           wait=lw, inc=("ldw%d" % (tt % 2), 16))
                for ch in range(2):
                    c0 = ch * 512
                    fw = [("sq", last_ev[0])] if last_ev[0] is not None else None
                    tok = qp.push(buf[:, c0:c0 + 512], wqt[:, c0:c0 + 512], 512,
                                  vtok=[("ldw%d" % (tt % 2), t_l)] if ch == 0
                                  else None, fin_wait=fw)
                    if tok is not None:
                        on_tok(prev_id[0], tok)
                    prev_id[0] = (tt, ch)
            fw = [("sq", last_ev[0])] if last_ev[0] is not None else None
            on_tok(prev_id[0], qp.flush(fin_wait=fw))
            prev_q[0] = qtoks[nrows_tiles - 1]
            return last_ev[0]

        prevq = [None]
        ev_g = prep_ffn_w(wg_in, wst, KI,
                          lambda t: wgT_a[:].rearrange(
                              "p (k f) -> p k f", f=FI)[:, :, t * 128:(t + 1) * 128],
                          prevq, None)
        ev_u = prep_ffn_w(wu_in, wst, KI,
                          lambda t: wuT_a[:].rearrange(
                              "p (k f) -> p k f", f=FI)[:, :, t * 128:(t + 1) * 128],
                          prevq, ev_g)
        t_spg = s.on("gpsimd", lambda e: e.dma_start(out=wgT_scr, in_=wgT_a[:]),
                     wait=[("sq", ev_g)], inc=("sww", 16))
        t_spu = s.on("gpsimd", lambda e: e.dma_start(out=wuT_scr, in_=wuT_a[:]),
                     wait=[("sq", ev_u)], inc=("sww", 16))

        def ln_stats(xbuf, pp, tok_x, wr_free=None, rs_free=None):
            def v_ln2(v):
                v.drain()
                v.bn_stats(lnb[:, 0:6], xbuf[:, 0:512])
                v.bn_stats(lnb[:, 6:12], xbuf[:, 512:1024])
                v.drain()
                v.bn_aggr(mvb[pp][:, 0:2], lnb[:, 0:12])
                v.drain()
                v.tensor_scalar(vpeb[pp][:], mvb[pp][:, 1:2], 1e-5, None, ALU.add)
                return v.tensor_scalar(negmeanb[pp][:], mvb[pp][:, 0:1], -1.0,
                                       None, ALU.mult)
            t_ln = s.on("vector", v_ln2, wait=list(tok_x) + (wr_free or []),
                        inc="vq")
            t_rs = s.on("gpsimd", lambda g, _p=pp: (g.drain(), g.tensor_tensor(
                rstdb[_p][:], vpeb[_p][:], nhalf[:], ALU.pow))[1],
                wait=[("vq", t_ln), ("gq", t_ginit)] + (rs_free or []), inc="gq")
            return t_ln, t_rs

        def ffn_phase(load_fn, store, wgT_use, wuT_use, hooks_fn, on_last,
                      tile_flush=False, nb=2):
            def fwp():
                ptag = qp.pending_tag
                if ptag is None:
                    return None
                kind, t_, c_ = ptag
                if kind == "xc" and c_ == 0 and ("evx", t_ - 1) in store:
                    return [("sq", store["evx", t_ - 1])]
                if kind == "hq" and c_ == 0 and ("hqfree", t_ - 1) in store:
                    return [store["hqfree", t_ - 1]]
                return None

            def route(r):
                if r is None:
                    return
                tag, tok = r
                kind, t_, c_ = tag
                if kind == "xc":
                    if c_ == 1:
                        w0 = [("vq", tok)]
                        if ("gulast", t_ - 2) in store:
                            w0.append(("tq", store["gulast", t_ - 2]))
                        store["evx", t_] = emit_trb(
                            [xqb[:, k * 128:(k + 1) * 128] for k in range(KD)],
                            xqTb[t_ % 2][:].rearrange("p (k m) -> p k m", m=128),
                            w0)
                else:
                    store["hq", t_, c_] = tok
                    if c_ == NGC - 1:
                        on_last(t_, tok)

            def pre(tt):
                pp = tt % 2
                t_x = load_fn(tt, x0b[pp],
                              [("sq", store["xc", tt - 2])]
                              if ("xc", tt - 2) in store else [])
                t_ln, t_rs = ln_stats(
                    x0b[pp], pp, [("ldx%d" % (tt % 2), t_x)],
                    wr_free=([("sq", store["xc", tt - 2])]
                             if ("xc", tt - 2) in store else []),
                    rs_free=([("sq", store["silu", tt - 2])]
                             if ("silu", tt - 2) in store else []))
                t_xc = s.on("scalar", lambda e, b=x0b[pp], _p=pp:
                            e.activation(xcb[:], b[:], AF.Identity,
                                         bias=negmeanb[_p][:, 0:1], scale=1.0),
                            wait=[("vq", t_ln)], inc="sq")
                store["xc", tt] = t_xc
                store["rs", tt] = t_rs
                route(qp.push(xcb[:, 0:512], xqb[:, 0:512], 512,
                              vtok=[("sq", t_xc)], clamp30=True,
                              tag=("xc", tt, 0), fin_wait=fwp()))
                route(qp.push(xcb[:, 512:1024], xqb[:, 512:1024], 512,
                              clamp30=True, tag=("xc", tt, 1), fin_wait=fwp()))

            def emit_mmsilu(tt, ci, prev_th):
                pp = tt % 2
                c0, cw = GUC[ci]
                gbank = ci % nb
                ubank = nb + ci % nb
                wts = [("sq", store["evx", tt])] if ci == 0 else []
                if ("th", tt, ci - nb) in store:
                    wts.append(("vq", store["th", tt, ci - nb]))
                if ci < nb and ("th", tt - 1, NGC - nb + ci) in store:
                    wts.append(("vq", store["th", tt - 1, NGC - nb + ci]))

                def mk_g(te, _c0=c0, _cw=cw, _gb=gbank, _pp=pp):
                    last = None
                    for k in range(KD):
                        last = te.matmul(
                            pbf[_gb][:, 0:_cw],
                            xqTb[_pp][:, k * 128:(k + 1) * 128],
                            wgT_use[:, k * FI + _c0:k * FI + _c0 + _cw],
                            start=(k == 0), stop=(k == KD - 1))
                    return last

                def mk_u(te, _c0=c0, _cw=cw, _ub=ubank, _pp=pp):
                    last = None
                    for k in range(KD):
                        last = te.matmul(
                            pbf[_ub][:, 0:_cw],
                            xqTb[_pp][:, k * 128:(k + 1) * 128],
                            wuT_use[:, k * FI + _c0:k * FI + _c0 + _cw],
                            start=(k == 0), stop=(k == KD - 1))
                    return last
                t_mg = s.on("tensor", mk_g, wait=wts, inc="tq")
                t_mu = s.on("tensor", mk_u, inc="tq")
                if ci == NGC - 1:
                    store["gulast", tt] = t_mu
                sw = [("tq", t_mg), ("gq", store["rs", tt])]
                if prev_th is not None:
                    sw.append(("vq", prev_th))
                t_sg = s.on("scalar", lambda e, _gb=gbank, _cw=cw, _pp=pp:
                            e.activation(sgb[:, 0:_cw], pbf[_gb][:, 0:_cw],
                                         AF.Silu, scale=rstdb[_pp][:, 0:1],
                                         bias=zb[:, 0:1]),
                            wait=sw, inc="sq")
                if ci == NGC - 1:
                    store["silu", tt] = t_sg
                return t_sg, t_mu, ubank

            pre(0)
            route(qp.flush(fin_wait=fwp()))
            nxt = emit_mmsilu(0, 0, None)
            for tt in range(NT):
                pp = tt % 2
                hooks = hooks_fn(tt) if hooks_fn else None
                for ci, (c0, cw) in enumerate(GUC):
                    t_sg, t_mu, ubank = nxt
                    t_h = s.on("vector", lambda v, _ub=ubank, _cw=cw, _pp=pp,
                               _sg=t_sg:
                               (v.drain(), v.scalar_tensor_tensor(
                                   hb[:, 0:_cw], sgb[:, 0:_cw],
                                   rstdb[_pp][:, 0:1], pbf[_ub][:, 0:_cw],
                                   ALU.mult, ALU.mult))[1],
                               wait=[("sq", t_sg), ("tq", t_mu)], inc="vq")
                    store["th", tt, ci] = t_h
                    if ci + 1 < NGC:
                        nxt = emit_mmsilu(tt, ci + 1, t_h)
                    route(qp.push(hb[:, 0:cw], hqb[:, c0:c0 + cw], cw,
                                  tag=("hq", tt, ci), fin_wait=fwp()))
                    if hooks and ci in hooks:
                        hooks[ci]()
                    if ci == 3 and tt + 1 < NT:
                        pre(tt + 1)
                    if ci + 1 == NGC:
                        if tile_flush:
                            route(qp.flush(fin_wait=fwp()))
                        if tt + 1 < NT:
                            nxt = emit_mmsilu(tt + 1, 0, t_h)
            route(qp.flush(fin_wait=fwp()))

        st1 = {}

        def loadx(r, buf, extra):
            return s.on("sync", lambda e, _r=r, _b=buf: e.dma_start(
                out=_b[:], in_=x_in[_r * 128:(_r + 1) * 128, :]),
                wait=extra, inc=("ldx%d" % (r % 2), 16))

        def on_last1(tt_, tok):
            t_sp = s.on("gpsimd", lambda e, r=tt_: e.dma_start(
                out=hq_scr[r * 128:(r + 1) * 128, :], in_=hqb[:]),
                wait=[("vq", tok)], inc=("swh", 16))
            st1["spill", tt_] = t_sp
            st1["hqfree", tt_] = ("swh", t_sp)

        ffn_phase(loadx, st1, wgT_a, wuT_a, None, on_last1, tile_flush=True,
                  nb=3)

        bar1 = s.barrier(dma_sems=DMAS)

        evd = [None]
        dq = {}
        prev_cid = [None]

        def dn_tok(cid, tok):
            tt_, ci_ = cid
            c0, cw = GUC[ci_]
            nk = cw // 128
            evd[0] = emit_trb(
                [xqb[:, k * 128:(k + 1) * 128] for k in range(nk)],
                wdT_b[:].rearrange("p (k f) -> p k f", f=D)[
                    :, c0 // 128:c0 // 128 + nk, tt_ * 128:(tt_ + 1) * 128],
                [("vq", tok)])
            if ci_ == NGC - 1:
                dq[tt_] = tok

        for tt in range(KD):
            buf = wdst[tt % 2]
            lw = [("ph", bar1)] if tt == 0 else []
            if tt - 2 in dq:
                lw.append(("vq", dq[tt - 2]))
            t_l = s.on("sync", lambda e, b=buf, r=tt: e.dma_start(
                out=b[:, 0:FI], in_=wd_in[r * 128:(r + 1) * 128, :]),
                wait=lw, inc=("ldw%d" % (tt % 2), 16))
            for ci, (c0, cw) in enumerate(GUC):
                fw = [("sq", evd[0])] if evd[0] is not None else None
                tok = qp.push(buf[:, c0:c0 + cw], xqb[:, 0:cw], cw,
                              vtok=[("ldw%d" % (tt % 2), t_l), ("ph", bar1)]
                              if ci == 0 else None, fin_wait=fw)
                if tok is not None:
                    dn_tok(prev_cid[0], tok)
                prev_cid[0] = (tt, ci)
        dn_tok(prev_cid[0], qp.flush(fin_wait=[("sq", evd[0])]))
        t_spd = s.on("gpsimd", lambda e: e.dma_start(out=wdT_scr, in_=wdT_b[:]),
                     wait=[("sq", evd[0])], inc=("sww", 16))

        prev_cast = None
        prev_trlast = None
        ev_q = None
        for p0 in range(NPAIR):
            lw = [("sq", prev_cast)] if prev_cast else [("vq", dq[KD - 1]), ("ph", bar1)]
            t_l = s.on("sync", lambda e, pp=p0: e.dma_start(
                out=wdst[0][:, 0:D], in_=wq_in[pp * 128:(pp + 1) * 128, :]),
                wait=lw, inc=("ldw0", 16))
            cw_ = [("ldw0", t_l)]
            if prev_trlast is not None:
                cw_.append(("tq", prev_trlast))
            t_c = s.on("scalar", lambda e: (e.drain(), e.copy(
                xqb[:, 0:D], wdst[0][:, 0:D]))[1], wait=cw_, inc="sq")
            prev_cast = t_c
            ev_q = emit_trb(
                [xqb[:, k * 128:(k + 1) * 128] for k in range(KD)],
                wqT[:, p0 * KD * 128:(p0 + 1) * KD * 128].rearrange(
                    "p (k m) -> p k m", m=128),
                [("sq", t_c)])
            prev_trlast = s.val("tq")
        ev_o = None
        for t in range(KD):
            t_l = s.on("sync", lambda e, ttt=t: e.dma_start(
                out=wdst[0][:, 0:H * 64], in_=wo_in[ttt * 128:(ttt + 1) * 128, :]),
                wait=[("sq", prev_cast)], inc=("ldw0", 16))
            t_c = s.on("scalar", lambda e: (e.drain(), e.copy(
                xqb[:, 0:H * 64], wdst[0][:, 0:H * 64]))[1],
                wait=[("ldw0", t_l), ("tq", prev_trlast)], inc="sq")
            prev_cast = t_c
            ev_o = emit_trb(
                [xqb[:, p0 * 128:(p0 + 1) * 128] for p0 in range(NPAIR)],
                woT[:].rearrange("p (k f) -> p k f", f=D)[
                    :, :, t * 128:(t + 1) * 128],
                [("sq", t_c)])
            prev_trlast = s.val("tq")
        ev_kv = None
        for k in range(KD):
            t_l1 = s.on("sync", lambda e, kk=k: e.dma_start(
                out=wdst[0][:, 0:64], in_=wv_in[kk * 128:(kk + 1) * 128, :]),
                wait=[("sq", prev_cast)], inc=("ldw0", 16))
            t_l2 = s.on("sync", lambda e, kk=k: e.dma_start(
                out=wdst[0][:, 64:128], in_=wk_in[kk * 128:(kk + 1) * 128, :]),
                inc=("ldw0", 16))
            ev_kv = s.on("scalar", lambda e, kk=k: (e.drain(), e.copy(
                wkv[:, kk * 128:(kk + 1) * 128], wdst[0][:, 0:128]))[1],
                wait=[("ldw0", t_l2)], inc="sq")
            prev_cast = ev_kv
        t_vp1 = None
        def g_vp(g):
            last = None
            for t in range(MT):
                last = g.memset(Vp[:, t * VSTR + 64:t * VSTR + 65], 1.0)
            return last
        t_vp1 = s.on("gpsimd", g_vp, wait=[("ph", bar1)], inc="gq")

        st2 = {}
        sp_all = st1["spill", NT - 1]

        def attn_block(tt):
            def mk_kv(te):
                last = None
                for k in range(KD):
                    last = te.matmul(pbf[2][:, 0:128],
                                     wkv[:, k * 128:(k + 1) * 128],
                                     ln2Tb[:, k * 128:(k + 1) * 128],
                                     start=(k == 0), stop=(k == KD - 1))
                return last
            kvw = [("sq", st2["evl", tt]), ("sq", ev_kv)]
            if ("kt", tt - 1) in st2:
                kvw.append(("sq", st2["kt", tt - 1]))
            t_kv = s.on("tensor", mk_kv, wait=kvw, inc="tq")

            def s_kv(e, r=tt):
                e.drain()
                e.copy(vtb[0:64, 0:128], pbf[2][0:64, 0:128])
                return e.activation(KTd[64:128, r * 128:(r + 1) * 128],
                                    pbf[2][64:128, 0:128], AF.Copy, scale=0.125)
            skw = [("tq", t_kv)]
            if ("vptr", tt - 1) in st2:
                skw.append(("sq", st2["vptr", tt - 1]))
            t_kt = s.on("scalar", s_kv, wait=skw, inc="sq")
            st2["kt", tt] = t_kt
            s.on("sync", lambda e, r=tt: e.dma_start(
                out=KTd[0:64, r * 128:(r + 1) * 128],
                in_=KTd[64:128, r * 128:(r + 1) * 128]),
                wait=[("sq", t_kt)], inc=("std", 16))
            st2["vptr", tt] = emit_tr(vtb[0:64, 0:128],
                                      Vp[:, tt * VSTR:tt * VSTR + 64],
                                      [("sq", t_kt), ("gq", t_vp1)])
            for p0 in range(NPAIR):
                qb = 3 if p0 % 2 == 0 else 5
                def mk_q(te, _p=p0, _qb=qb):
                    last = None
                    for k in range(KD):
                        last = te.matmul(
                            pbf[_qb][:, 0:128],
                            wqT[:, (_p * KD + k) * 128:(_p * KD + k + 1) * 128],
                            ln2Tb[:, k * 128:(k + 1) * 128],
                            start=(k == 0), stop=(k == KD - 1))
                    return last
                qw = []
                if p0 == 0:
                    qw += [("sq", st2["evl", tt]), ("sq", ev_q)]
                prevqe = st2.get(("qe", tt, p0 - 2),
                                 st2.get(("qe", tt - 1, NPAIR - 2 + p0)))
                if prevqe is not None:
                    qw.append(("sq", prevqe))
                t_qm = s.on("tensor", mk_q, wait=qw, inc="tq")
                qew = [("tq", t_qm)]
                if p0 == 0:
                    prevqs = st2.get(("qs", tt - 1))
                    if prevqs is not None:
                        qew.append(("stq", prevqs))
                t_qe = s.on("scalar", lambda e, _p=p0, _qb=qb: (e.drain(), e.copy(
                    qpba[:, _p * 128:(_p + 1) * 128], pbf[_qb][:, 0:128]))[1],
                    wait=qew, inc="sq")
                st2["qe", tt, p0] = t_qe
            t_qs = s.on("sync", lambda e, r=tt: e.dma_start(
                out=q_scr[:, :, r * 128:(r + 1) * 128].rearrange(
                    "q p m -> p q m"),
                in_=qpba[:].rearrange("p (q m) -> p q m", m=128)),
                wait=[("sq", st2["qe", tt, NPAIR - 1])], inc=("stq", 16))
            st2["qs", tt] = t_qs
            t_q2 = s.on("vector", lambda v: (v.drain(), v.tensor_tensor(
                q2b[:, 0:NPAIR * 128], qpba[:], qpba[:], ALU.mult))[1],
                wait=[("sq", st2["qe", tt, NPAIR - 1]), ("stq", t_qs)], inc="vq")
            nw = [("vq", t_q2), ("ldc", t_cst)]
            prevqmax = st2.get(("qmax", tt - 1))
            if prevqmax is not None:
                nw.append(("vq", prevqmax))
            t_nmm = s.on("tensor", lambda te: te.matmul(
                pbf[4][0:2, 0:512], onesblk[:, 0:2], q2b[:, 0:512],
                start=True, stop=True), wait=nw, inc="tq")
            t_qx1 = s.on("vector", lambda v: (v.drain(), v.reduce_max(
                cbuf[0:2, 0:1], pbf[4][0:2, 0:512], axis=AX.X), v.drain(),
                v.tensor_tensor(qmaxb[0:2, 0:1], qmaxb[0:2, 0:1],
                                cbuf[0:2, 0:1], ALU.max))[-1],
                wait=[("tq", t_nmm)], inc="vq")
            t_nmm2 = s.on("tensor", lambda te: te.matmul(
                pbf[4][0:2, 0:512], onesblk[:, 0:2], q2b[:, 512:1024],
                start=True, stop=True), wait=[("vq", t_qx1)], inc="tq")
            t_qmax = s.on("vector", lambda v: (v.drain(), v.reduce_max(
                cbuf[0:2, 0:1], pbf[4][0:2, 0:512], axis=AX.X), v.drain(),
                v.tensor_tensor(qmaxb[0:2, 0:1], qmaxb[0:2, 0:1],
                                cbuf[0:2, 0:1], ALU.max))[-1],
                wait=[("tq", t_nmm2)], inc="vq")
            st2["qmax", tt] = t_qmax

        for tt in range(NT):
            pp = tt % 2
            lhw = [("ph", bar1)]
            if tt >= 1:
                lhw.append(("sq", st2["hqT", tt - 1]))
            t_lh = s.on("sync", lambda e, r=tt: e.dma_start(
                out=hqb[:], in_=hq_scr[r * 128:(r + 1) * 128, :]),
                wait=lhw, inc=("ldh", 16))
            lxw = [("ph", bar1)] if tt < 2 else [("vq", st2["x1", tt - 2])]
            t_lx = s.on("sync", lambda e, r=tt, b=x0b[pp]: e.dma_start(
                out=b[:], in_=x_in[r * 128:(r + 1) * 128, :]),
                wait=lxw, inc=("ldx%d" % (tt % 2), 16))
            evh = None
            k0 = 0
            first = True
            while k0 < KI:
                nk = min(8, KI - k0)
                w0 = [("ldh", t_lh), ("sq", evd[0])] if first else []
                if first and ("dn", tt - 1) in st2:
                    w0.append(("tq", st2["dn", tt - 1]))
                first = False
                evh = emit_trb(
                    [hqb[:, (k0 + k) * 128:(k0 + k + 1) * 128] for k in range(nk)],
                    hqTb[:, k0 * 128:(k0 + nk) * 128].rearrange(
                        "p (k m) -> p k m", m=128), w0)
                k0 += nk
            st2["hqT", tt] = evh

            def mk_dn(te):
                last = None
                for hi in range(2):
                    for k in range(KI):
                        last = te.matmul(
                            pbf[hi][:, 0:512],
                            hqTb[:, k * 128:(k + 1) * 128],
                            wdT_b[:, k * D + hi * 512:k * D + hi * 512 + 512],
                            start=(k == 0), stop=(k == KI - 1))
                return last
            wdn = [("sq", evh)]
            if ("x1", tt - 1) in st2:
                wdn.append(("vq", st2["x1", tt - 1]))
            t_dn = s.on("tensor", mk_dn, wait=wdn, inc="tq")
            st2["dn", tt] = t_dn

            def mk_x1(v, _pp=pp):
                v.drain()
                v.scalar_tensor_tensor(x1b[_pp][:, 0:512], pbf[0][:, 0:512],
                                       0.5, x0b[_pp][:, 0:512], ALU.mult, ALU.add)
                v.drain()
                return v.scalar_tensor_tensor(x1b[_pp][:, 512:1024],
                                              pbf[1][:, 0:512], 0.5,
                                              x0b[_pp][:, 512:1024],
                                              ALU.mult, ALU.add)
            x1w = [("tq", t_dn), ("ldx%d" % (tt % 2), t_lx)]
            if ("lb", tt - 2) in st2:
                x1w.append(("sq", st2["lb", tt - 2]))
            if ("sx1", tt - 2) in st2:
                x1w.append(("swx%d" % (tt % 2), st2["sx1", tt - 2]))
            t_x1 = s.on("vector", mk_x1, wait=x1w, inc="vq")
            st2["x1", tt] = t_x1
            t_sx1 = s.on("gpsimd", lambda e, r=tt, b=x1b[pp]: e.dma_start(
                out=x1_scr[r * 128:(r + 1) * 128, :], in_=b[:]),
                wait=[("vq", t_x1)], inc=("swx%d" % (tt % 2), 16))
            st2["sx1", tt] = t_sx1
            t_ln2, t_rs2 = ln_stats(
                x1b[pp], pp, [("vq", t_x1)],
                wr_free=([("sq", st2["lb", tt - 2])] if ("lb", tt - 2) in st2 else []),
                rs_free=([("sq", st2["lb", tt - 2])] if ("lb", tt - 2) in st2 else []))
            t_nm = s.on("vector", lambda v, _p=pp: (v.drain(), v.tensor_scalar(
                nmrb[_p][:], negmeanb[_p][:], rstdb[_p][:, 0:1], None, ALU.mult))[1],
                wait=[("gq", t_rs2)], inc="vq")
            if tt >= 1:
                attn_block(tt - 1)
            lbw = [("vq", t_nm)]
            if ("qe", tt - 1, NPAIR - 1) in st2:
                lbw.append(("sq", st2["qe", tt - 1, NPAIR - 1]))
            t_lb = s.on("scalar", lambda e, b=x1b[pp], _p=pp: e.activation(
                ln2bb[:], b[:], AF.Identity, scale=rstdb[_p][:, 0:1],
                bias=nmrb[_p][:, 0:1]), wait=lbw, inc="sq")
            st2["lb", tt] = t_lb
            evl = emit_trb([ln2bb[:, k * 128:(k + 1) * 128] for k in range(KD)],
                           ln2Tb[:].rearrange("p (k m) -> p k m", m=128),
                           [("sq", t_lb)])
            st2["evl", tt] = evl
        attn_block(NT - 1)

        last_qmax = st2["qmax", NT - 1]
        last_dup = s.val("std")
        t_kprev = None
        for j in range((T + 511) // 512):
            kw = min(512, T - j * 512)
            t_k2 = s.on("vector", lambda v, _j=j, _w=kw: (v.drain(),
                v.tensor_tensor(k2b[:, 0:_w], KTd[:, _j * 512:_j * 512 + _w],
                                KTd[:, _j * 512:_j * 512 + _w], ALU.mult))[1],
                wait=([("std", last_dup), ("vq", last_qmax)] if j == 0 else []),
                inc="vq")
            t_km = s.on("tensor", lambda te, _w=kw: te.matmul(
                pbf[4][0:2, 0:_w], onesblk[:, 0:2], k2b[:, 0:_w],
                start=True, stop=True),
                wait=[("vq", t_k2)] + ([("vq", t_kprev)] if t_kprev else []),
                inc="tq")
            t_kx = s.on("vector", lambda v, _w=kw: (v.drain(), v.reduce_max(
                cbuf[0:2, 1:2], pbf[4][0:2, 0:_w], axis=AX.X), v.drain(),
                v.tensor_tensor(cbuf[0:2, 2:3], cbuf[0:2, 2:3], cbuf[0:2, 1:2],
                                ALU.max))[-1],
                wait=[("tq", t_km)], inc="vq")
            t_kprev = t_kx
        t_m1 = s.on("sync", lambda e: e.dma_start(out=cbuf[0:1, 4:5],
                                                  in_=qmaxb[1:2, 0:1]),
                    wait=[("vq", t_kprev)], inc=("st", 16))
        t_m2 = s.on("sync", lambda e: e.dma_start(out=cbuf[0:1, 5:6],
                                                  in_=cbuf[1:2, 2:3]),
                    wait=[("st", t_m1)], inc=("st", 16))

        def v_c(v):
            v.drain()
            v.tensor_tensor(cbuf[0:1, 6:7], qmaxb[0:1, 0:1], cbuf[0:1, 4:5],
                            ALU.max)
            v.tensor_tensor(cbuf[0:1, 7:8], cbuf[0:1, 2:3], cbuf[0:1, 5:6],
                            ALU.max)
            v.drain()
            return v.tensor_tensor(cbuf[0:1, 8:9], cbuf[0:1, 6:7],
                                   cbuf[0:1, 7:8], ALU.mult)
        t_vc = s.on("vector", v_c, wait=[("st", t_m2)], inc="vq")
        t_pc = s.on("gpsimd", lambda g: (g.drain(), g.tensor_tensor(
            cbuf[0:1, 9:10], cbuf[0:1, 8:9], half[0:1, 0:1], ALU.pow))[1],
            wait=[("vq", t_vc)], inc="gq")
        TEMP_BISECT = True
        t_bc = s.on("tensor", lambda te: te.matmul(
            pbf[4][:, 0:1], onesc[0:1, 0:128], cbuf[0:1, 9:10],
            start=True, stop=True), wait=[("gq", t_pc)], inc="tq")
        t_negc = s.on("scalar", lambda e: e.activation(
            negc[:], pbf[4][:, 0:1], AF.Copy, scale=-1.0),
            wait=[("tq", t_bc)], inc="sq")
        s.on("scalar", lambda e: e.drain())

        p2 = {}
        bankfree = {}

        for cc in range(NCH):
            lxw = [("ph", bar1)]
            if ("x2v", cc - 2) in p2:
                lxw.append(("vq", p2["x2v", cc - 2]))
            t_lx1 = s.on("sync", lambda e, _c=cc: e.dma_start(
                out=x1cb[_c % 2][:].rearrange("p (t d) -> p t d", t=CHT),
                in_=x1_scr.rearrange("(t p) d -> p t d", p=128)[
                    :, _c * CHT:(_c + 1) * CHT, :]),
                wait=lxw + [("swx%d" % ((NT - 2) % 2), st2["sx1", NT - 2]),
                            ("swx%d" % ((NT - 1) % 2), st2["sx1", NT - 1]),
                            ("sww", t_spd)] if cc == 0 else lxw,
                inc=("ldx0", 16))
            lq_last = None
            for p0 in range(NPAIR):
                lqw = []
                if cc == 0 and p0 == 0:
                    lqw.append(("stq", st2["qs", NT - 1]))
                if p0 == 0 and ("simlast", cc - 1) in p2:
                    lqw.append(("tq", p2["simlast", cc - 1]))
                lq_last = s.on("sync", lambda e, _c=cc, _p=p0: e.dma_start(
                    out=qchb[:, _p * CH:(_p + 1) * CH],
                    in_=q_scr[_p, :, _c * CH:(_c + 1) * CH]),
                    wait=lqw, inc=("ldq", 16))
            for p0 in range(NPAIR):
                par = (cc + p0) % 2
                bp = 4 if par == 0 else 6
                pvA = pbf[bp]
                pvB = pbf[bp + 1]
                pvW = pbwf[bp // 2]

                def emit_sim(mt):
                    sb0 = (mt % 2) * 2

                    def mk_sim(te, _m=mt, _b=sb0, _p=p0):
                        te.matmul(pbwf[_b // 2][:, 0:CH],
                                  KTd[0:64, _m * 128:(_m + 1) * 128],
                                  qchb[0:64, _p * CH:(_p + 1) * CH],
                                  start=True, stop=True)
                        return te.matmul(pbwf[_b // 2][:, CH:2 * CH],
                                         KTd[64:128, _m * 128:(_m + 1) * 128],
                                         qchb[64:128, _p * CH:(_p + 1) * CH],
                                         start=True, stop=True)
                    w = []
                    if mt == 0 and p0 == 0:
                        w += [("sq", t_negc), ("ldq", lq_last)]
                    pe = p2.get(("exp", cc, p0, mt - 2))
                    if pe is None and mt < 2:
                        pe = p2.get(("exp", cc, p0 - 1, MT - 2 + mt)) or \
                            p2.get(("exp", cc - 1, NPAIR - 1, MT - 2 + mt))
                    if pe is not None:
                        w.append(("sq", pe))
                    t_sim = s.on("tensor", mk_sim, wait=w, inc="tq")
                    if mt == MT - 1:
                        p2["simlast", cc] = t_sim

                    def mk_exp(e, _b=sb0, _pp=(mt % 4)):
                        return e.activation(attb[_pp][:, 0:2 * CH],
                                            pbwf[_b // 2][:, 0:2 * CH],
                                            AF.Exp, bias=negc[:, 0:1])
                    we = [("tq", t_sim)]
                    pv = p2.get(("pv", cc, p0, mt - 4))
                    if pv is None and mt < 4:
                        mt2 = mt + 4 * ((MT - 1 - mt) // 4)
                        pv = p2.get(("pv", cc, p0 - 1, mt2)) or \
                            p2.get(("pv", cc - 1, NPAIR - 1, mt2))
                    if pv is not None:
                        we.append(("tq", pv))
                    t_exp = s.on("scalar", mk_exp, wait=we, inc="sq")
                    p2["exp", cc, p0, mt] = t_exp

                def emit_pv(mt):
                    def mk_pv(te, _m=mt, _pp=(mt % 4), _A=pvA, _B=pvB):
                        te.matmul(_A[0:65, 0:CH],
                                  Vp[:, _m * VSTR:_m * VSTR + 65],
                                  attb[_pp][:, 0:CH],
                                  start=(_m == 0), stop=(_m == MT - 1))
                        return te.matmul(_B[0:65, 0:CH],
                                         Vp[:, _m * VSTR:_m * VSTR + 65],
                                         attb[_pp][:, CH:2 * CH],
                                         start=(_m == 0), stop=(_m == MT - 1))
                    wp = [("sq", p2["exp", cc, p0, mt])]
                    if mt == 0 and par in bankfree:
                        wp += bankfree[par]
                    t_pv = s.on("tensor", mk_pv, wait=wp, inc="tq")
                    p2["pv", cc, p0, mt] = t_pv
                    return t_pv

                for mt in range(MT):
                    emit_sim(mt)
                    if mt >= 2:
                        emit_pv(mt - 2)
                if MT >= 2:
                    emit_pv(MT - 2)
                pv_last = emit_pv(MT - 1)

                ssw = [("tq", pv_last)]
                prev_s2 = p2.get(("s2", cc, p0 - 1)) or \
                    p2.get(("s2", cc - 1, NPAIR - 1))
                if prev_s2 is not None:
                    ssw.append(("st", prev_s2))
                t_sa = s.on("scalar", lambda e, _w=pvW: (e.drain(), e.copy(
                    sumsb[64:65, 0:2 * CH].rearrange("p (k c) -> p k c", c=CH),
                    _w[64:65, :].rearrange("p (k c) -> p k c", k=2)[:, :, 0:CH]))[1],
                    wait=ssw, inc="sq")
                s1w = [("sq", t_sa)]
                prev_b2w = p2.get(("b2", cc, p0 - 1)) or \
                    p2.get(("b2", cc - 1, NPAIR - 1))
                if prev_b2w is not None:
                    s1w.append(("st", prev_b2w))
                t_s1 = s.on("sync", lambda e: e.dma_start(
                    out=rsum2[0:1, 0:CH], in_=sumsb[64:65, 0:CH]),
                    wait=s1w, inc=("st", 16))
                t_s2 = s.on("sync", lambda e: e.dma_start(
                    out=rsum2[1:2, 0:CH], in_=sumsb[64:65, CH:2 * CH]),
                    wait=[("st", t_s1)], inc=("st", 16))
                p2["s2", cc, p0] = t_s2
                t_rc = s.on("vector", lambda v: (v.drain(), v.reciprocal(
                    rsum2[0:2, 0:CH], rsum2[0:2, 0:CH]))[1],
                    wait=[("st", t_s2)], inc="vq")
                sdw = [("vq", t_rc)]
                prev_b2 = p2.get(("b2", cc, p0 - 1)) or \
                    p2.get(("b2", cc - 1, NPAIR - 1))
                if prev_b2 is not None:
                    sdw.append(("st", prev_b2))
                t_sd = s.on("sync", lambda e: e.dma_start(
                    out=rs_scr, in_=rsum2[0:2, 0:CH]), wait=sdw, inc=("st", 16))
                obw0 = []
                prev_ob = p2.get(("ob", cc, p0 - 1)) or \
                    p2.get(("ob", cc - 1, NPAIR - 1))
                if prev_ob is not None:
                    obw0.append(("vq", prev_ob))
                t_b1 = s.on("sync", lambda e: e.dma_start(
                    out=rsbb[0:64, 0:CH].unsqueeze(1),
                    in_=rs_scr[0:1, :].partition_broadcast(64)),
                    wait=[("st", t_sd)] + obw0, inc=("st", 16))
                t_b2 = s.on("sync", lambda e: e.dma_start(
                    out=rsbb[64:128, 0:CH].unsqueeze(1),
                    in_=rs_scr[1:2, :].partition_broadcast(64)),
                    wait=[("st", t_b1)], inc=("st", 16))
                p2["b2", cc, p0] = t_b2
                t_oa = s.on("vector", lambda v, _p=p0, _a=pvA: (v.drain(),
                            v.tensor_tensor(
                                ostk[0:64, _p * CH:(_p + 1) * CH],
                                _a[0:64, 0:CH], rsbb[0:64, 0:CH],
                                ALU.mult))[1],
                            wait=[("st", t_b2)], inc="vq")
                obw = []
                prev_oh = p2.get(("oh", cc, p0 - 1)) or \
                    p2.get(("oh", cc - 1, NPAIR - 1))
                if prev_oh is not None:
                    obw.append(("sto", prev_oh))
                t_ob = s.on("vector", lambda v, _b=pvB: (v.drain(), v.tensor_tensor(
                    oBb[0:64, 0:CH], _b[0:64, 0:CH], rsbb[64:128, 0:CH],
                    ALU.mult))[1], wait=obw, inc="vq")
                p2["ob", cc, p0] = t_ob
                t_oh = s.on("sync", lambda e, _p=p0: e.dma_start(
                    out=ostk[64:128, _p * CH:(_p + 1) * CH],
                    in_=oBb[0:64, 0:CH]),
                    wait=[("vq", t_ob)], inc=("sto", 16))
                p2["oh", cc, p0] = t_oh
                p2["oa", cc, p0] = t_oa
                bankfree[par] = [("vq", t_ob)]
            prj_par = (cc + 6) % 2
            prjA = pbf[4 if prj_par == 0 else 6]
            prjB = pbf[(4 if prj_par == 0 else 6) + 1]
            t_prev = None
            for si in range(CHT):
                def mk_prj(te, _s=si, _A=prjA, _B=prjB):
                    last = None
                    for p0 in range(NPAIR):
                        for hh in range(2):
                            last = te.matmul(
                                (_A if hh == 0 else _B)[:, 0:512],
                                ostk[:, p0 * CH + _s * 128:p0 * CH + (_s + 1) * 128],
                                woT[:, p0 * D + hh * 512:p0 * D + (hh + 1) * 512],
                                start=(p0 == 0), stop=(p0 == NPAIR - 1))
                    return last
                wpr = [("sto", p2["oh", cc, NPAIR - 1]),
                       ("vq", p2["oa", cc, NPAIR - 1]),
                       ("vq", p2["ob", cc, NPAIR - 1])]
                if si == 0 and prj_par in bankfree:
                    wpr += bankfree[prj_par]
                if t_prev is not None:
                    wpr.append(("vq", t_prev))
                t_pr = s.on("tensor", mk_prj, wait=wpr, inc="tq")

                def mk_x2(v, _s=si, _A=prjA, _B=prjB, _c=cc):
                    v.drain()
                    v.tensor_tensor(x2cb[_c % 2][:, _s * D:_s * D + 512],
                                    x1cb[_c % 2][:, _s * D:_s * D + 512],
                                    _A[:, 0:512], ALU.add)
                    v.drain()
                    return v.tensor_tensor(
                        x2cb[_c % 2][:, _s * D + 512:_s * D + 1024],
                        x1cb[_c % 2][:, _s * D + 512:_s * D + 1024],
                        _B[:, 0:512], ALU.add)
                xw = [("tq", t_pr), ("ldx0", t_lx1)]
                if si == 0 and ("x2s", cc - 2) in p2:
                    xw.append(("swc", p2["x2s", cc - 2]))
                t_x2 = s.on("vector", mk_x2, wait=xw, inc="vq")
                t_prev = t_x2
            p2["x2v", cc] = t_prev
            bankfree[prj_par] = [("vq", t_prev)]
            t_sx2 = s.on("gpsimd", lambda e, _c=cc: e.dma_start(
                out=x2_scr.rearrange("(t p) d -> p t d", p=128)[
                    :, _c * CHT:(_c + 1) * CHT, :],
                in_=x2cb[_c % 2][:].rearrange("p (t d) -> p t d", t=CHT)),
                wait=[("vq", t_prev)], inc=("swc", 16))
            p2["x2s", cc] = t_sx2

        bar2 = s.barrier(dma_sems=DMAS)

        t_rg = s.on("sync", lambda e: e.dma_start(out=wgT_c, in_=wgT_scr),
                    wait=[("ph", bar2)], inc=("ldw0", 16))
        t_ru = s.on("scalar", lambda e: e.dma_start(out=wuT_c, in_=wuT_scr),
                    wait=[("ph", bar2)], inc=("ldw0", 16))
        t_rd = s.on("gpsimd", lambda e: e.dma_start(out=wdT_c, in_=wdT_scr),
                    wait=[("ph", bar2)], inc=("sww", 16))
        t_rdw = s.val("sww")

        st3 = {}

        def loadx2(r, buf, extra):
            w = list(extra)
            if r == 0:
                w += [("ldw0", t_ru), ("sww", t_rdw),
                      ("swc", p2["x2s", NCH - 1])]
            if ("outv", r - 2) in st3:
                w.append(("vq", st3["outv", r - 2]))
            return s.on("sync", lambda e, _r=r, _b=buf: e.dma_start(
                out=_b[:], in_=x2_scr[_r * 128:(_r + 1) * 128, :]),
                wait=w, inc=("ldx%d" % (r % 2), 16))

        def p3_tr(tt):
            tokh = st3["hq", tt, NGC - 1]
            evh = None
            k0 = 0
            first = True
            while k0 < KI:
                nk = min(8, KI - k0)
                w0 = [("vq", tokh)] if first else []
                if first and ("dn", tt - 1) in st3:
                    w0.append(("tq", st3["dn", tt - 1]))
                first = False
                evh = emit_trb(
                    [hqb[:, (k0 + k) * 128:(k0 + k + 1) * 128] for k in range(nk)],
                    hqTb[:, k0 * 128:(k0 + nk) * 128].rearrange(
                        "p (k m) -> p k m", m=128), w0)
                k0 += nk
            st3["hqfree", tt] = ("sq", evh)
            st3["evh", tt] = evh

        def p3_down_part(tt, hi, k0, k1):
            def mk_dn(te, _hi=hi, _k0=k0, _k1=k1):
                last = None
                for k in range(_k0, _k1):
                    last = te.matmul(
                        pbf[4 + _hi][:, 0:512],
                        hqTb[:, k * 128:(k + 1) * 128],
                        wdT_c[:, k * D + _hi * 512:k * D + _hi * 512 + 512],
                        start=(k == 0), stop=(k == KI - 1))
                return last
            wdn = []
            if k0 == 0:
                wdn.append(("sq", st3["evh", tt]))
                if ("outv", tt - 1) in st3:
                    wdn.append(("vq", st3["outv", tt - 1]))
            t_dn = s.on("tensor", mk_dn, wait=wdn, inc="tq")
            if hi == 1 and k1 == KI:
                st3["dn", tt] = t_dn

        def p3_out(tt, pp):
            def mk_out(v, _pp=pp):
                v.drain()
                v.scalar_tensor_tensor(x1b[_pp][:, 0:512], pbf[4][:, 0:512],
                                       0.5, x0b[_pp][:, 0:512],
                                       ALU.mult, ALU.add)
                v.drain()
                return v.scalar_tensor_tensor(x1b[_pp][:, 512:1024],
                                              pbf[5][:, 0:512], 0.5,
                                              x0b[_pp][:, 512:1024],
                                              ALU.mult, ALU.add)
            ow = [("tq", st3["dn", tt])]
            if ("so", tt - 2) in st3:
                ow.append(("so%d" % (tt % 2), st3["so", tt - 2]))
            t_out = s.on("vector", mk_out, wait=ow, inc="vq")
            st3["outv", tt] = t_out
            t_so = s.on("gpsimd", lambda e, r=tt, b=x1b[pp]: e.dma_start(
                out=out_hbm[r * 128:(r + 1) * 128, :], in_=b[:]),
                wait=[("vq", t_out)], inc=("so%d" % (tt % 2), 16))
            st3["so", tt] = t_so

        KH = (KI + 1) // 2

        def hooks3(tt):
            if tt == 0:
                return None
            pt = tt - 1
            ppt = pt % 2
            return {0: (lambda _t=pt: p3_down_part(_t, 0, 0, KH)),
                    1: (lambda _t=pt: p3_down_part(_t, 0, KH, KI)),
                    2: (lambda _t=pt: p3_down_part(_t, 1, 0, KH)),
                    3: (lambda _t=pt, _p=ppt: (
                        p3_down_part(_t, 1, KH, KI), p3_out(_t, _p))[1])}

        def on_last3(tt_, tok):
            p3_tr(tt_)

        ffn_phase(loadx2, st3, wgT_c, wuT_c, hooks3, on_last3, tile_flush=True)
        p3_down_part(NT - 1, 0, 0, KI)
        p3_down_part(NT - 1, 1, 0, KI)
        p3_out(NT - 1, (NT - 1) % 2)

        s.on("sync", lambda e: e.nop(),
             wait=[("so0", s.val("so0")), ("so1", s.val("so1"))])
        s.play(block)
    return nc


def make_consts():
    ident = np.eye(P, dtype=ml_dtypes.bfloat16)
    onesblk = np.zeros((P, 2), dtype=np.float32)
    onesblk[0:64, 0] = 1.0
    onesblk[64:128, 1] = 1.0
    ones2 = np.zeros((2, P), dtype=np.float32)
    ones2[0, 0:64] = 1.0
    ones2[1, 64:128] = 1.0
    onesc = np.ones((1, P), dtype=np.float32)
    return {"identc": ident, "onesblkc": onesblk, "ones2c": ones2,
            "onescc": onesc}


def kernel(L0_attn_x, L0_attn_query_proj, L0_attn_key_proj, L0_attn_value_proj,
           L0_attn_output_proj, L0_ffn_w_gate, L0_ffn_w_up, L0_ffn_w_down):
    x = np.ascontiguousarray(np.asarray(L0_attn_x, dtype=np.float32))
    B, T, Dm = x.shape
    wq = np.ascontiguousarray(
        np.asarray(L0_attn_query_proj, dtype=np.float32).reshape(H * 64, Dm))
    wk = np.ascontiguousarray(np.asarray(L0_attn_key_proj, dtype=np.float32))
    wv = np.ascontiguousarray(np.asarray(L0_attn_value_proj, dtype=np.float32))
    wo = np.ascontiguousarray(
        np.asarray(L0_attn_output_proj, dtype=np.float32).reshape(Dm, H * 64))
    wg = np.ascontiguousarray(np.asarray(L0_ffn_w_gate, dtype=np.float32))
    wu = np.ascontiguousarray(np.asarray(L0_ffn_w_up, dtype=np.float32))
    wd = np.ascontiguousarray(np.asarray(L0_ffn_w_down, dtype=np.float32))
    consts = make_consts()
    nc = build_program(T=T)
    in_maps = []
    for c in range(B):
        m = {"x": np.ascontiguousarray(x[c]), "w_gate": wg, "w_up": wu,
             "w_down": wd, "q_proj": wq, "k_proj": wk, "v_proj": wv,
             "o_proj": wo}
        m.update(consts)
        in_maps.append(m)
    res = run_bass_kernel_spmd(nc, in_maps, core_ids=list(range(B)))
    out = np.stack([r["out"] for r in res.results], axis=0)
    return out.astype(np.float32)


# version 25 (final)
# speedup vs baseline: 1.2007x; 1.2007x over previous
import numpy as np
from contextlib import ExitStack
import concourse.bass as bass
import concourse.mybir as mybir
from concourse.bass_utils import run_bass_kernel_spmd
import ml_dtypes

F32 = mybir.dt.float32
I32 = mybir.dt.int32
BF16 = mybir.dt.bfloat16
AF = mybir.ActivationFunctionType
ALU = mybir.AluOpType
AX = mybir.AxisListType

P = 128
D = 1024
FI = 2816
H = 16
NPAIR = H // 2
KD = D // 128
KI = FI // 128
VSTR = 80

ENGINES = ("sync", "vector", "scalar", "tensor", "gpsimd")
BAR_ENGINES = ("sync", "vector", "scalar", "tensor", "gpsimd")


class Sched:
    def __init__(self, nc, es):
        self.nc = nc
        self.es = es
        self.q = {e: [] for e in ENGINES}
        self.sems = {}
        self.counts = {}
        self.flush = {}

    def sem(self, name):
        self.sems[name] = self.es.enter_context(self.nc.semaphore(name))
        self.counts[name] = 0
        return name

    def val(self, name):
        return self.counts[name]

    def on(self, eng, fn, wait=(), inc=None, amt=1):
        after = None
        wait = [w for w in wait if w is not None and w[1] is not None]
        if inc is not None:
            if isinstance(inc, tuple):
                inc, amt = inc
            before = self.counts[inc]
            self.counts[inc] += amt
            after = self.counts[inc]
            if amt == 16 and after - self.flush.get(inc, 0) > 128:
                wait.append((inc, before))
                self.flush[inc] = before
        self.q[eng].append((fn, tuple(wait), inc, amt))
        return after

    def barrier(self, dma_sems=()):
        for e in BAR_ENGINES:
            if e == "sync":
                w = [(nm, self.counts[nm]) for nm in dma_sems]
                self.on(e, lambda en: en.nop(), wait=w, inc="ph")
            else:
                self.on(e, lambda en: (en.drain(), en.nop())[1], inc="ph")
        return self.val("ph")

    def play(self, block):
        def run(e, name):
            for fn, waits, inc, amt in self.q[name]:
                for (sname, v) in waits:
                    e.wait_ge(self.sems[sname], v)
                r = fn(e)
                if inc is not None:
                    assert r is not None, f"{name} op inc={inc} returned None"
                    r.then_inc(self.sems[inc], amt)
        for name in ENGINES:
            if not self.q[name]:
                continue
            getattr(block, name)(lambda e, n=name: run(e, n))


class QPipe:

    def __init__(self, s, t, t2):
        self.s = s
        self.sets = [t, t2]
        self.sh = t
        self.pend = None
        self.idx = 0

    @property
    def pending_tag(self):
        return self.pend[4] if self.pend else None

    def push(self, x_ap, xq_out, C, vtok=None, clamp30=False, fin_wait=None,
             tag=None):
        s = self.s
        p = self.idx & 1
        self.idx += 1
        ts = self.sets[p]
        b1, bmax, hs, rs = ts["b1"], ts["bmax"], ts["hs"], ts["rs"]
        b2 = self.sh["b2"]
        B = C // 16
        x3 = x_ap.rearrange("p (b k) -> p b k", k=16)
        clamp = 1e-30 if clamp30 else 1e-12

        def v_head(v, x3=x3, b1=b1, bmax=bmax, hs=hs, rs=rs, B=B, C=C,
                   clamp=clamp):
            v.drain()
            v.reduce_max(bmax[:, :B], x3, axis=AX.X, apply_absolute_value=True)
            v.drain()
            v.tensor_scalar(bmax[:, :B], bmax[:, :B], 1.0 / 6.0, clamp,
                            ALU.mult, ALU.max)
            v.drain()
            v.tensor_scalar(hs[:, :B], bmax[:, :B], 0.5, None, ALU.mult)
            v.reciprocal(rs[:, :B], bmax[:, :B])
            v.drain()
            rsb = rs[:, :B].unsqueeze(2).broadcast_to([128, B, 16])
            return v.scalar_tensor_tensor(
                b1[:, :C].rearrange("p (b k) -> p b k", k=16), x3, 1.0, rsb,
                ALU.mult, ALU.mult)

        xv = s.on("vector", v_head, wait=(vtok or ()), inc="vq")
        sqv = s.on("scalar", lambda e, b2=b2, b1=b1, C=C:
                   e.activation(b2[:, :C], b1[:, :C], AF.Square),
                   wait=[("vq", xv + 1 if self.pend else xv)], inc="sq")
        tok = self._retire(fin_wait) if self.pend else None
        self.pend = (p, xq_out, C, sqv, tag)
        return tok

    def _retire(self, fin_wait=None):
        s = self.s
        p, xq_out, C, sqv, tag = self.pend
        self.pend = None
        ts = self.sets[p]
        b1, hs = ts["b1"], ts["hs"]
        sh = self.sh
        b2, b3, b4, b5, ri = sh["b2"], sh["b3"], sh["b4"], sh["b5"], sh["ri"]
        B = C // 16

        def v_mid(v, b1=b1, C=C):
            v.drain()
            v.tensor_scalar(b3[:, :C], b2[:, :C], 4.0, -1.0, ALU.is_ge, ALU.mult)
            v.tensor_scalar(b4[:, :C], b2[:, :C], 16.0, -0.5, ALU.is_ge, ALU.mult)
            v.drain()
            v.scalar_tensor_tensor(b5[:, :C], b3[:, :C], 2.0, b4[:, :C],
                                   ALU.add, ALU.add)
            v.drain()
            return v.scalar_tensor_tensor(b1[:, :C], b1[:, :C], 1.0, b5[:, :C],
                                          ALU.mult, ALU.mult)

        uv = s.on("vector", v_mid, wait=[("sq", sqv)], inc="vq")
        rv = s.on("scalar", lambda e, b1=b1, C=C:
                  e.activation(ri[:, :C], b1[:, :C], AF.Copy,
                               scale=32768.0, bias=64.5 * 32768.0),
                  wait=[("vq", uv)], inc="sq")

        def v_tail(v, hs=hs, C=C, B=B):
            v.scalar_tensor_tensor(b4[:, :C], b4[:, :C], -4.0, b3[:, :C],
                                   ALU.mult, ALU.subtract)
            v.drain()
            hsb = hs[:, :B].unsqueeze(2).broadcast_to([128, B, 16])
            v.scalar_tensor_tensor(b3[:, :C].rearrange("p (b k) -> p b k", k=16),
                                   b4[:, :C].rearrange("p (b k) -> p b k", k=16),
                                   1.0, hsb, ALU.add, ALU.mult)

        s.on("vector", v_tail)

        def v_final(v, xq_out=xq_out, C=C):
            v.tensor_scalar(ri[:, :C], ri[:, :C], 15, None,
                            ALU.arith_shift_right)
            v.drain()
            return v.scalar_tensor_tensor(xq_out[:, :C], ri[:, :C], -64.0,
                                          b3[:, :C], ALU.add, ALU.mult)

        qv = s.on("vector", v_final, wait=[("sq", rv)] + list(fin_wait or []),
                  inc="vq")
        return (tag, qv) if tag is not None else qv

    def flush(self, fin_wait=None):
        return self._retire(fin_wait) if self.pend else None


class SAlloc:

    BASE = 24576
    LIMIT = 224 * 1024 - 64

    def __init__(self, nc):
        self.nc = nc
        self.perm_top = self.BASE
        self.phase_base = None
        self.cur = None
        self.uid = 0

    def _alloc(self, name, shape, dt, off):
        nbytes = int(np.prod(shape[1:])) * {F32: 4, I32: 4, BF16: 2}[dt]
        nbytes = (nbytes + 31) & ~31
        self.uid += 1
        h = self.nc.alloc_sbuf_tensor_at("%s_u%d" % (name, self.uid), shape, dt,
                                         offset=off)
        return h.ap(), off + nbytes

    def perm(self, name, shape, dt=F32):
        assert self.phase_base is None, "perm alloc after phase start"
        ap_, self.perm_top = self._alloc(name, shape, dt, self.perm_top)
        return ap_

    def perm_alias(self, name, shape, dt, off):
        ap_, _ = self._alloc(name, shape, dt, off)
        return ap_

    def start_phases(self):
        self.phase_base = (self.perm_top + 63) & ~63

    def region(self):
        self.cur = self.phase_base
        return self

    def ph(self, name, shape, dt=F32):
        ap_, self.cur = self._alloc(name, shape, dt, self.cur)
        assert self.cur <= self.LIMIT, f"sbuf phase overflow {self.cur}"
        return ap_

    def sub(self):
        return self.cur

    def set(self, off):
        self.cur = off


def build_program(T=4096, CH=512):
    NT = T // 128
    NCH = T // CH
    CHT = CH // 128
    MT = T // 128
    GUC = []
    c = 0
    while c < FI:
        w = min(512, FI - c)
        GUC.append((c, w))
        c += w
    NGC = len(GUC)

    nc = bass.Bass("TRN2", target_bir_lowering=False, debug=False)
    x_in = nc.dram_tensor("x", [T, D], F32, kind="ExternalInput").ap()
    wg_in = nc.dram_tensor("w_gate", [FI, D], F32, kind="ExternalInput").ap()
    wu_in = nc.dram_tensor("w_up", [FI, D], F32, kind="ExternalInput").ap()
    wd_in = nc.dram_tensor("w_down", [D, FI], F32, kind="ExternalInput").ap()
    wq_in = nc.dram_tensor("q_proj", [H * 64, D], F32, kind="ExternalInput").ap()
    wk_in = nc.dram_tensor("k_proj", [D, 64], F32, kind="ExternalInput").ap()
    wv_in = nc.dram_tensor("v_proj", [D, 64], F32, kind="ExternalInput").ap()
    wo_in = nc.dram_tensor("o_proj", [D, H * 64], F32, kind="ExternalInput").ap()
    ident_in = nc.dram_tensor("identc", [P, P], BF16, kind="ExternalInput").ap()
    onesblk_in = nc.dram_tensor("onesblkc", [P, 2], F32, kind="ExternalInput").ap()
    ones2_in = nc.dram_tensor("ones2c", [2, P], F32, kind="ExternalInput").ap()
    onesc_in = nc.dram_tensor("onescc", [1, P], F32, kind="ExternalInput").ap()
    out_hbm = nc.dram_tensor("out", [T, D], F32, kind="ExternalOutput").ap()
    x1_scr = nc.dram_tensor("x1scr", [T, D], F32, kind="Internal").ap()
    x2_scr = nc.dram_tensor("x2scr", [T, D], F32, kind="Internal").ap()
    hq_scr = nc.dram_tensor("hqscr", [T, FI], BF16, kind="Internal").ap()
    q_scr = nc.dram_tensor("qscr", [NPAIR, P, T], BF16, kind="Internal").ap()
    wgT_scr = nc.dram_tensor("wgTscr", [P, KD * FI], BF16, kind="Internal").ap()
    wuT_scr = nc.dram_tensor("wuTscr", [P, KD * FI], BF16, kind="Internal").ap()
    wdT_scr = nc.dram_tensor("wdTscr", [P, KI * D], BF16, kind="Internal").ap()
    rs_scr = nc.dram_tensor("rsscr", [2, CH], F32, kind="Internal").ap()

    es = ExitStack()
    with es:
        A = SAlloc(nc)
        ident = A.perm("ident", [P, P], BF16)
        onesblk = A.perm("onesblk", [P, 2])
        ones2 = A.perm("ones2", [2, P])
        onesc = A.perm("onesc", [1, P])
        half = A.perm("half", [P, 1])
        zb = A.perm("zb", [P, 1])
        nhalf = A.perm("nhalf", [P, 1])
        qmaxb = A.perm("qmaxb", [P, 1])
        cbuf = A.perm("cbuf", [P, 16])
        negc = A.perm("negc", [P, 1])
        mvb = [A.perm("mv%d" % i, [P, 2]) for i in range(2)]
        vpeb = [A.perm("vpe%d" % i, [P, 1]) for i in range(2)]
        rstdb = [A.perm("rstd%d" % i, [P, 1]) for i in range(2)]
        negmeanb = [A.perm("negmean%d" % i, [P, 1]) for i in range(2)]
        nmrb = [A.perm("nmr%d" % i, [P, 1]) for i in range(2)]
        x0b = [A.perm("x0b%d" % i, [P, D]) for i in range(2)]
        xcb = A.perm("xcb", [P, D])
        xqb = A.perm("xqb", [P, D], BF16)
        xqTb = [A.perm("xqTb%d" % i, [P, KD * 128], BF16) for i in range(2)]
        sgb = A.perm("sgb", [P, 512])
        hb = A.perm("hb", [P, 512])
        hqb = A.perm("hqb", [P, FI], BF16)
        hqTb = A.perm("hqTb", [P, KI * 128], BF16)
        x1b = [A.perm("x1b%d" % i, [P, D]) for i in range(2)]
        tmp = {k: A.perm("t_" + k, [P, 512]) for k in
               ("b1", "b2", "b3", "b4")}
        b5_off = A.perm_top
        tmp["b5"] = A.perm("t_b5", [P, 512])
        tmp["ri"] = A.perm_alias("t_ri", [P, 512], I32, b5_off)
        for k in ("bmax", "hs", "rs"):
            tmp[k] = A.perm("t_" + k, [P, 32])
        tmp2 = {"b1": A.perm("t2_b1", [P, 512])}
        for k in ("bmax", "hs", "rs"):
            tmp2[k] = A.perm("t2_" + k, [P, 32])
        lnb = A.perm("lnb", [P, 12])
        ln2bb = A.perm("ln2bb", [P, D], BF16)
        ln2Tb = A.perm("ln2Tb", [P, KD * 128], BF16)
        qpb = A.perm("qpb", [P, 128], BF16)
        vtb = A.perm("vtb", [P, 128], BF16)
        q2b = A.perm("q2b", [P, NPAIR * 128])
        k2b = A.perm("k2b", [P, 512])
        A.start_phases()

        A.region()
        wgT_a = A.ph("wgT_a", [P, KD * FI], BF16)
        wuT_a = A.ph("wuT_a", [P, KD * FI], BF16)
        wst = [A.ph("wst%d" % i, [P, D]) for i in range(2)]
        wqt = A.ph("wqt", [P, D], BF16)

        A.region()
        KTd = A.ph("KTd", [P, T], BF16)
        Vp = A.ph("Vp", [P, MT * VSTR], BF16)
        woT = A.ph("woT", [P, NPAIR * D], BF16)
        subB = A.sub()
        wqT = A.ph("wqT", [P, NPAIR * KD * 128], BF16)
        wkv = A.ph("wkv", [P, KD * 128], BF16)
        wdT_b = A.ph("wdT_b", [P, KI * D], BF16)
        wdst = [A.ph("wdst%d" % i, [P, FI]) for i in range(2)]
        qpba = A.ph("qpba", [P, NPAIR * 128], BF16)
        A.set(subB)
        x1cb = [A.ph("x1cb%d" % i, [P, CHT * D]) for i in range(2)]
        qchb = A.ph("qchb", [P, NPAIR * CH], BF16)
        attb = [A.ph("attb%d" % i, [P, 2 * CH], BF16) for i in range(4)]
        ostk = A.ph("ostk", [P, NPAIR * CH], BF16)
        x2cb = [A.ph("x2cb%d" % i, [P, CHT * D]) for i in range(2)]
        oBb = A.ph("oBb", [P, CH], BF16)
        sumsb = A.ph("sumsb", [P, 2 * CH])
        rsum2 = A.ph("rsum2", [P, CH])
        rsbb = A.ph("rsbb", [P, CH])

        A.region()
        wgT_c = A.ph("wgT_c", [P, KD * FI], BF16)
        wuT_c = A.ph("wuT_c", [P, KD * FI], BF16)
        wdT_c = A.ph("wdT_c", [P, KI * D], BF16)

        PBW = [es.enter_context(nc.psum_tensor("pbw%d" % i, [P, 1024], F32))
               for i in range(4)]
        pbwf = [b.ap() for b in PBW]
        pbf = [PBW[i // 2].ap()[:, (i % 2) * 512:(i % 2) * 512 + 512]
               for i in range(8)]
        pbh = [PBW[i // 2].ap().bitcast(BF16)[:, (i % 2) * 1024:(i % 2) * 1024 + 1024]
               for i in range(8)]

        s = Sched(nc, es)
        for nm in ("ldw0", "ldw1", "ldc", "ldx0", "ldx1", "ldh", "ldq",
                   "vq", "sq", "tq", "gq", "swh", "swx0", "swx1", "sww",
                   "swc", "st", "stq", "std", "sto", "so0", "so1", "ph"):
            s.sem(nm)
        DMAS = ("ldw0", "ldw1", "ldc", "ldx0", "ldx1", "ldh", "ldq", "swh",
                "swx0", "swx1", "sww", "swc", "st", "stq", "std", "sto",
                "so0", "so1")
        block = es.enter_context(nc.Block())
        qp = QPipe(s, tmp, tmp2)

        s.on("sync", lambda e: e.dma_start(out=ident[:], in_=ident_in), inc=("ldc", 16))
        s.on("sync", lambda e: e.dma_start(out=onesblk[:, 0:2], in_=onesblk_in), inc=("ldc", 16))
        s.on("sync", lambda e: e.dma_start(out=ones2[0:2, 0:128], in_=ones2_in), inc=("ldc", 16))
        t_cst = s.on("sync", lambda e: e.dma_start(out=onesc[0:1, 0:128], in_=onesc_in), inc=("ldc", 16))

        def g_init(g):
            g.memset(half[0:1, 0:1], 0.5)
            g.memset(zb[:, 0:1], 0.0)
            g.memset(nhalf[:, 0:1], -0.5)
            g.memset(qmaxb[0:2, 0:1], 0.0)
            return g.memset(cbuf[0:2, 2:3], 0.0)
        t_ginit = s.on("gpsimd", g_init, inc="gq")

        trst = {"n": 0, "ev": {}}

        def emit_tr(src_ap, dst_ap, wait_tok=None):
            b = 6 + (trst["n"] % 2)
            wdt = src_ap.shape[0]
            w = list(wait_tok) if wait_tok else []
            if trst["n"] == 0:
                w.append(("ldc", t_cst))
            if trst["n"] >= 2:
                w.append(("sq", trst["ev"][trst["n"] - 2]))
            tt = s.on("tensor", lambda te, bb=b, sp=src_ap, ww=wdt: te.transpose(
                pbh[bb][:, 0:ww], sp, ident[0:ww, 0:ww]), wait=w, inc="tq")
            ev = s.on("scalar", lambda e, bb=b, dd=dst_ap, ww=wdt: e.copy(
                dd, pbh[bb][:, 0:ww]), wait=[("tq", tt)], inc="sq")
            trst["ev"][trst["n"]] = ev
            trst["n"] += 1
            return ev

        def emit_trb(srcs, dst_ap, wait_tok=None):
            n = len(srcs)
            b = 6 + (trst["n"] % 2)
            w = list(wait_tok) if wait_tok else []
            if trst["n"] == 0:
                w.append(("ldc", t_cst))
            if trst["n"] >= 2:
                w.append(("sq", trst["ev"][trst["n"] - 2]))

            def f(te, _srcs=srcs, _b=b):
                last = None
                for k, sp in enumerate(_srcs):
                    last = te.transpose(pbh[_b][:, k * 128:(k + 1) * 128], sp,
                                        ident[:])
                return last
            tt = s.on("tensor", f, wait=w, inc="tq")
            ev = s.on("scalar", lambda e, _b=b, dd=dst_ap, _n=n: e.copy(
                dd, pbh[_b][:, 0:_n * 128].rearrange("p (k m) -> p k m", m=128)),
                wait=[("tq", tt)], inc="sq")
            trst["ev"][trst["n"]] = ev
            trst["n"] += 1
            return ev

        def prep_ffn_w(w_hbm, stage, nrows_tiles, dst_of, prev_q, init_ev):
            last_ev = [init_ev]
            qtoks = {}
            prev_id = [None]

            def on_tok(pid, tok):
                tt_, ch_ = pid
                if ch_ == 1:
                    qtoks[tt_] = tok
                    last_ev[0] = emit_trb(
                        [wqt[:, k * 128:(k + 1) * 128] for k in range(KD)],
                        dst_of(tt_), [("vq", tok)])

            for tt in range(nrows_tiles):
                buf = stage[tt % 2]
                lw = []
                if tt - 2 in qtoks:
                    lw.append(("vq", qtoks[tt - 2]))
                elif tt < 2 and prev_q[0] is not None:
                    lw.append(("vq", prev_q[0]))
                t_l = s.on("sync", lambda e, b=buf, r=tt, _w=w_hbm:
                           e.dma_start(out=b[:, 0:D],
                                       in_=_w[r * 128:(r + 1) * 128, :]),
                           wait=lw, inc=("ldw%d" % (tt % 2), 16))
                for ch in range(2):
                    c0 = ch * 512
                    fw = [("sq", last_ev[0])] if last_ev[0] is not None else None
                    tok = qp.push(buf[:, c0:c0 + 512], wqt[:, c0:c0 + 512], 512,
                                  vtok=[("ldw%d" % (tt % 2), t_l)] if ch == 0
                                  else None, fin_wait=fw)
                    if tok is not None:
                        on_tok(prev_id[0], tok)
                    prev_id[0] = (tt, ch)
            fw = [("sq", last_ev[0])] if last_ev[0] is not None else None
            on_tok(prev_id[0], qp.flush(fin_wait=fw))
            prev_q[0] = qtoks[nrows_tiles - 1]
            return last_ev[0]

        prevq = [None]
        ev_g = prep_ffn_w(wg_in, wst, KI,
                          lambda t: wgT_a[:].rearrange(
                              "p (k f) -> p k f", f=FI)[:, :, t * 128:(t + 1) * 128],
                          prevq, None)
        ev_u = prep_ffn_w(wu_in, wst, KI,
                          lambda t: wuT_a[:].rearrange(
                              "p (k f) -> p k f", f=FI)[:, :, t * 128:(t + 1) * 128],
                          prevq, ev_g)
        t_spg = s.on("gpsimd", lambda e: e.dma_start(out=wgT_scr, in_=wgT_a[:]),
                     wait=[("sq", ev_g)], inc=("sww", 16))
        t_spu = s.on("gpsimd", lambda e: e.dma_start(out=wuT_scr, in_=wuT_a[:]),
                     wait=[("sq", ev_u)], inc=("sww", 16))

        def ln_stats(xbuf, pp, tok_x, wr_free=None, rs_free=None):
            def v_ln2(v):
                v.drain()
                v.bn_stats(lnb[:, 0:6], xbuf[:, 0:512])
                v.bn_stats(lnb[:, 6:12], xbuf[:, 512:1024])
                v.drain()
                v.bn_aggr(mvb[pp][:, 0:2], lnb[:, 0:12])
                v.drain()
                v.tensor_scalar(vpeb[pp][:], mvb[pp][:, 1:2], 1e-5, None, ALU.add)
                return v.tensor_scalar(negmeanb[pp][:], mvb[pp][:, 0:1], -1.0,
                                       None, ALU.mult)
            t_ln = s.on("vector", v_ln2, wait=list(tok_x) + (wr_free or []),
                        inc="vq")
            t_rs = s.on("gpsimd", lambda g, _p=pp: (g.drain(), g.tensor_tensor(
                rstdb[_p][:], vpeb[_p][:], nhalf[:], ALU.pow))[1],
                wait=[("vq", t_ln), ("gq", t_ginit)] + (rs_free or []), inc="gq")
            return t_ln, t_rs

        def ffn_phase(load_fn, store, wgT_use, wuT_use, hooks_fn, on_last,
                      tile_flush=False, nb=2):
            def fwp():
                ptag = qp.pending_tag
                if ptag is None:
                    return None
                kind, t_, c_ = ptag
                if kind == "xc" and c_ == 0 and ("evx", t_ - 1) in store:
                    return [("sq", store["evx", t_ - 1])]
                if kind == "hq" and c_ == 0 and ("hqfree", t_ - 1) in store:
                    return [store["hqfree", t_ - 1]]
                return None

            def route(r):
                if r is None:
                    return
                tag, tok = r
                kind, t_, c_ = tag
                if kind == "xc":
                    if c_ == 1:
                        w0 = [("vq", tok)]
                        if ("gulast", t_ - 2) in store:
                            w0.append(("tq", store["gulast", t_ - 2]))
                        store["evx", t_] = emit_trb(
                            [xqb[:, k * 128:(k + 1) * 128] for k in range(KD)],
                            xqTb[t_ % 2][:].rearrange("p (k m) -> p k m", m=128),
                            w0)
                else:
                    store["hq", t_, c_] = tok
                    if c_ == NGC - 1:
                        on_last(t_, tok)

            def pre(tt):
                pp = tt % 2
                t_x = load_fn(tt, x0b[pp],
                              [("sq", store["xc", tt - 2])]
                              if ("xc", tt - 2) in store else [])
                t_ln, t_rs = ln_stats(
                    x0b[pp], pp, [("ldx%d" % (tt % 2), t_x)],
                    wr_free=([("sq", store["xc", tt - 2])]
                             if ("xc", tt - 2) in store else []),
                    rs_free=([("sq", store["silu", tt - 2])]
                             if ("silu", tt - 2) in store else []))
                t_xc = s.on("scalar", lambda e, b=x0b[pp], _p=pp:
                            e.activation(xcb[:], b[:], AF.Identity,
                                         bias=negmeanb[_p][:, 0:1], scale=1.0),
                            wait=[("vq", t_ln)], inc="sq")
                store["xc", tt] = t_xc
                store["rs", tt] = t_rs
                route(qp.push(xcb[:, 0:512], xqb[:, 0:512], 512,
                              vtok=[("sq", t_xc)], clamp30=True,
                              tag=("xc", tt, 0), fin_wait=fwp()))
                route(qp.push(xcb[:, 512:1024], xqb[:, 512:1024], 512,
                              clamp30=True, tag=("xc", tt, 1), fin_wait=fwp()))

            def emit_mmsilu(tt, ci, prev_th):
                pp = tt % 2
                c0, cw = GUC[ci]
                gbank = ci % nb
                ubank = nb + ci % nb
                wts = [("sq", store["evx", tt])] if ci == 0 else []
                if ("th", tt, ci - nb) in store:
                    wts.append(("vq", store["th", tt, ci - nb]))
                if ci < nb and ("th", tt - 1, NGC - nb + ci) in store:
                    wts.append(("vq", store["th", tt - 1, NGC - nb + ci]))

                def mk_g(te, _c0=c0, _cw=cw, _gb=gbank, _pp=pp):
                    last = None
                    for k in range(KD):
                        last = te.matmul(
                            pbf[_gb][:, 0:_cw],
                            xqTb[_pp][:, k * 128:(k + 1) * 128],
                            wgT_use[:, k * FI + _c0:k * FI + _c0 + _cw],
                            start=(k == 0), stop=(k == KD - 1))
                    return last

                def mk_u(te, _c0=c0, _cw=cw, _ub=ubank, _pp=pp):
                    last = None
                    for k in range(KD):
                        last = te.matmul(
                            pbf[_ub][:, 0:_cw],
                            xqTb[_pp][:, k * 128:(k + 1) * 128],
                            wuT_use[:, k * FI + _c0:k * FI + _c0 + _cw],
                            start=(k == 0), stop=(k == KD - 1))
                    return last
                t_mg = s.on("tensor", mk_g, wait=wts, inc="tq")
                t_mu = s.on("tensor", mk_u, inc="tq")
                if ci == NGC - 1:
                    store["gulast", tt] = t_mu
                sw = [("tq", t_mg), ("gq", store["rs", tt])]
                if prev_th is not None:
                    sw.append(("vq", prev_th))
                t_sg = s.on("scalar", lambda e, _gb=gbank, _cw=cw, _pp=pp:
                            e.activation(sgb[:, 0:_cw], pbf[_gb][:, 0:_cw],
                                         AF.Silu, scale=rstdb[_pp][:, 0:1],
                                         bias=zb[:, 0:1]),
                            wait=sw, inc="sq")
                if ci == NGC - 1:
                    store["silu", tt] = t_sg
                return t_sg, t_mu, ubank

            pre(0)
            route(qp.flush(fin_wait=fwp()))
            nxt = emit_mmsilu(0, 0, None)
            for tt in range(NT):
                pp = tt % 2
                hooks = hooks_fn(tt) if hooks_fn else None
                for ci, (c0, cw) in enumerate(GUC):
                    t_sg, t_mu, ubank = nxt
                    t_h = s.on("vector", lambda v, _ub=ubank, _cw=cw, _pp=pp,
                               _sg=t_sg:
                               (v.drain(), v.scalar_tensor_tensor(
                                   hb[:, 0:_cw], sgb[:, 0:_cw],
                                   rstdb[_pp][:, 0:1], pbf[_ub][:, 0:_cw],
                                   ALU.mult, ALU.mult))[1],
                               wait=[("sq", t_sg), ("tq", t_mu)], inc="vq")
                    store["th", tt, ci] = t_h
                    if ci + 1 < NGC:
                        nxt = emit_mmsilu(tt, ci + 1, t_h)
                    route(qp.push(hb[:, 0:cw], hqb[:, c0:c0 + cw], cw,
                                  tag=("hq", tt, ci), fin_wait=fwp()))
                    if hooks and ci in hooks:
                        hooks[ci]()
                    if ci == 3 and tt + 1 < NT:
                        pre(tt + 1)
                    if ci + 1 == NGC:
                        if tile_flush:
                            route(qp.flush(fin_wait=fwp()))
                        if tt + 1 < NT:
                            nxt = emit_mmsilu(tt + 1, 0, t_h)
            route(qp.flush(fin_wait=fwp()))

        st1 = {}

        def loadx(r, buf, extra):
            return s.on("sync", lambda e, _r=r, _b=buf: e.dma_start(
                out=_b[:], in_=x_in[_r * 128:(_r + 1) * 128, :]),
                wait=extra, inc=("ldx%d" % (r % 2), 16))

        def on_last1(tt_, tok):
            t_sp = s.on("gpsimd", lambda e, r=tt_: e.dma_start(
                out=hq_scr[r * 128:(r + 1) * 128, :], in_=hqb[:]),
                wait=[("vq", tok)], inc=("swh", 16))
            st1["spill", tt_] = t_sp
            st1["hqfree", tt_] = ("swh", t_sp)

        ffn_phase(loadx, st1, wgT_a, wuT_a, None, on_last1, tile_flush=True,
                  nb=3)

        bar1 = s.barrier(dma_sems=DMAS)

        evd = [None]
        dq = {}
        prev_cid = [None]

        def dn_tok(cid, tok):
            tt_, ci_ = cid
            c0, cw = GUC[ci_]
            nk = cw // 128
            evd[0] = emit_trb(
                [xqb[:, k * 128:(k + 1) * 128] for k in range(nk)],
                wdT_b[:].rearrange("p (k f) -> p k f", f=D)[
                    :, c0 // 128:c0 // 128 + nk, tt_ * 128:(tt_ + 1) * 128],
                [("vq", tok)])
            if ci_ == NGC - 1:
                dq[tt_] = tok

        for tt in range(KD):
            buf = wdst[tt % 2]
            lw = [("ph", bar1)] if tt == 0 else []
            if tt - 2 in dq:
                lw.append(("vq", dq[tt - 2]))
            t_l = s.on("sync", lambda e, b=buf, r=tt: e.dma_start(
                out=b[:, 0:FI], in_=wd_in[r * 128:(r + 1) * 128, :]),
                wait=lw, inc=("ldw%d" % (tt % 2), 16))
            for ci, (c0, cw) in enumerate(GUC):
                fw = [("sq", evd[0])] if evd[0] is not None else None
                tok = qp.push(buf[:, c0:c0 + cw], xqb[:, 0:cw], cw,
                              vtok=[("ldw%d" % (tt % 2), t_l), ("ph", bar1)]
                              if ci == 0 else None, fin_wait=fw)
                if tok is not None:
                    dn_tok(prev_cid[0], tok)
                prev_cid[0] = (tt, ci)
        dn_tok(prev_cid[0], qp.flush(fin_wait=[("sq", evd[0])]))
        t_spd = s.on("gpsimd", lambda e: e.dma_start(out=wdT_scr, in_=wdT_b[:]),
                     wait=[("sq", evd[0])], inc=("sww", 16))

        prev_cast = None
        prev_trlast = None
        ev_q = None
        for p0 in range(NPAIR):
            lw = [("sq", prev_cast)] if prev_cast else [("vq", dq[KD - 1]), ("ph", bar1)]
            t_l = s.on("sync", lambda e, pp=p0: e.dma_start(
                out=wdst[0][:, 0:D], in_=wq_in[pp * 128:(pp + 1) * 128, :]),
                wait=lw, inc=("ldw0", 16))
            cw_ = [("ldw0", t_l)]
            if prev_trlast is not None:
                cw_.append(("tq", prev_trlast))
            t_c = s.on("scalar", lambda e: (e.drain(), e.copy(
                xqb[:, 0:D], wdst[0][:, 0:D]))[1], wait=cw_, inc="sq")
            prev_cast = t_c
            ev_q = emit_trb(
                [xqb[:, k * 128:(k + 1) * 128] for k in range(KD)],
                wqT[:, p0 * KD * 128:(p0 + 1) * KD * 128].rearrange(
                    "p (k m) -> p k m", m=128),
                [("sq", t_c)])
            prev_trlast = s.val("tq")
        ev_o = None
        for t in range(KD):
            t_l = s.on("sync", lambda e, ttt=t: e.dma_start(
                out=wdst[0][:, 0:H * 64], in_=wo_in[ttt * 128:(ttt + 1) * 128, :]),
                wait=[("sq", prev_cast)], inc=("ldw0", 16))
            t_c = s.on("scalar", lambda e: (e.drain(), e.copy(
                xqb[:, 0:H * 64], wdst[0][:, 0:H * 64]))[1],
                wait=[("ldw0", t_l), ("tq", prev_trlast)], inc="sq")
            prev_cast = t_c
            ev_o = emit_trb(
                [xqb[:, p0 * 128:(p0 + 1) * 128] for p0 in range(NPAIR)],
                woT[:].rearrange("p (k f) -> p k f", f=D)[
                    :, :, t * 128:(t + 1) * 128],
                [("sq", t_c)])
            prev_trlast = s.val("tq")
        ev_kv = None
        for k in range(KD):
            t_l1 = s.on("sync", lambda e, kk=k: e.dma_start(
                out=wdst[0][:, 0:64], in_=wv_in[kk * 128:(kk + 1) * 128, :]),
                wait=[("sq", prev_cast)], inc=("ldw0", 16))
            t_l2 = s.on("sync", lambda e, kk=k: e.dma_start(
                out=wdst[0][:, 64:128], in_=wk_in[kk * 128:(kk + 1) * 128, :]),
                inc=("ldw0", 16))
            ev_kv = s.on("scalar", lambda e, kk=k: (e.drain(), e.copy(
                wkv[:, kk * 128:(kk + 1) * 128], wdst[0][:, 0:128]))[1],
                wait=[("ldw0", t_l2)], inc="sq")
            prev_cast = ev_kv
        t_vp1 = None
        def g_vp(g):
            last = None
            for t in range(MT):
                last = g.memset(Vp[:, t * VSTR + 64:t * VSTR + 65], 1.0)
            return last
        t_vp1 = s.on("gpsimd", g_vp, wait=[("ph", bar1)], inc="gq")

        st2 = {}
        sp_all = st1["spill", NT - 1]

        def attn_block(tt):
            def mk_kv(te):
                last = None
                for k in range(KD):
                    last = te.matmul(pbf[2][:, 0:128],
                                     wkv[:, k * 128:(k + 1) * 128],
                                     ln2Tb[:, k * 128:(k + 1) * 128],
                                     start=(k == 0), stop=(k == KD - 1))
                return last
            kvw = [("sq", st2["evl", tt]), ("sq", ev_kv)]
            if ("kt", tt - 1) in st2:
                kvw.append(("sq", st2["kt", tt - 1]))
            t_kv = s.on("tensor", mk_kv, wait=kvw, inc="tq")

            def s_kv(e, r=tt):
                e.drain()
                e.copy(vtb[0:64, 0:128], pbf[2][0:64, 0:128])
                return e.activation(KTd[64:128, r * 128:(r + 1) * 128],
                                    pbf[2][64:128, 0:128], AF.Copy, scale=0.125)
            skw = [("tq", t_kv)]
            if ("vptr", tt - 1) in st2:
                skw.append(("sq", st2["vptr", tt - 1]))
            t_kt = s.on("scalar", s_kv, wait=skw, inc="sq")
            st2["kt", tt] = t_kt
            s.on("sync", lambda e, r=tt: e.dma_start(
                out=KTd[0:64, r * 128:(r + 1) * 128],
                in_=KTd[64:128, r * 128:(r + 1) * 128]),
                wait=[("sq", t_kt)], inc=("std", 16))
            st2["vptr", tt] = emit_tr(vtb[0:64, 0:128],
                                      Vp[:, tt * VSTR:tt * VSTR + 64],
                                      [("sq", t_kt), ("gq", t_vp1)])
            for p0 in range(NPAIR):
                qb = 3 if p0 % 2 == 0 else 5
                def mk_q(te, _p=p0, _qb=qb):
                    last = None
                    for k in range(KD):
                        last = te.matmul(
                            pbf[_qb][:, 0:128],
                            wqT[:, (_p * KD + k) * 128:(_p * KD + k + 1) * 128],
                            ln2Tb[:, k * 128:(k + 1) * 128],
                            start=(k == 0), stop=(k == KD - 1))
                    return last
                qw = []
                if p0 == 0:
                    qw += [("sq", st2["evl", tt]), ("sq", ev_q)]
                prevqe = st2.get(("qe", tt, p0 - 2),
                                 st2.get(("qe", tt - 1, NPAIR - 2 + p0)))
                if prevqe is not None:
                    qw.append(("sq", prevqe))
                t_qm = s.on("tensor", mk_q, wait=qw, inc="tq")
                qew = [("tq", t_qm)]
                if p0 == 0:
                    prevqs = st2.get(("qs", tt - 1))
                    if prevqs is not None:
                        qew.append(("stq", prevqs))
                t_qe = s.on("scalar", lambda e, _p=p0, _qb=qb: (e.drain(), e.copy(
                    qpba[:, _p * 128:(_p + 1) * 128], pbf[_qb][:, 0:128]))[1],
                    wait=qew, inc="sq")
                st2["qe", tt, p0] = t_qe
            t_qs = s.on("sync", lambda e, r=tt: e.dma_start(
                out=q_scr[:, :, r * 128:(r + 1) * 128].rearrange(
                    "q p m -> p q m"),
                in_=qpba[:].rearrange("p (q m) -> p q m", m=128)),
                wait=[("sq", st2["qe", tt, NPAIR - 1])], inc=("stq", 16))
            st2["qs", tt] = t_qs
            t_q2 = s.on("vector", lambda v: (v.drain(), v.tensor_tensor(
                q2b[:, 0:NPAIR * 128], qpba[:], qpba[:], ALU.mult))[1],
                wait=[("sq", st2["qe", tt, NPAIR - 1]), ("stq", t_qs)], inc="vq")
            nw = [("vq", t_q2), ("ldc", t_cst)]
            prevqmax = st2.get(("qmax", tt - 1))
            if prevqmax is not None:
                nw.append(("vq", prevqmax))
            t_nmm = s.on("tensor", lambda te: te.matmul(
                pbf[4][0:2, 0:512], onesblk[:, 0:2], q2b[:, 0:512],
                start=True, stop=True), wait=nw, inc="tq")
            t_qx1 = s.on("vector", lambda v: (v.drain(), v.reduce_max(
                cbuf[0:2, 0:1], pbf[4][0:2, 0:512], axis=AX.X), v.drain(),
                v.tensor_tensor(qmaxb[0:2, 0:1], qmaxb[0:2, 0:1],
                                cbuf[0:2, 0:1], ALU.max))[-1],
                wait=[("tq", t_nmm)], inc="vq")
            t_nmm2 = s.on("tensor", lambda te: te.matmul(
                pbf[4][0:2, 0:512], onesblk[:, 0:2], q2b[:, 512:1024],
                start=True, stop=True), wait=[("vq", t_qx1)], inc="tq")
            t_qmax = s.on("vector", lambda v: (v.drain(), v.reduce_max(
                cbuf[0:2, 0:1], pbf[4][0:2, 0:512], axis=AX.X), v.drain(),
                v.tensor_tensor(qmaxb[0:2, 0:1], qmaxb[0:2, 0:1],
                                cbuf[0:2, 0:1], ALU.max))[-1],
                wait=[("tq", t_nmm2)], inc="vq")
            st2["qmax", tt] = t_qmax

        for tt in range(NT):
            pp = tt % 2
            lhw = [("ph", bar1)]
            if tt >= 1:
                lhw.append(("sq", st2["hqT", tt - 1]))
            t_lh = s.on("sync", lambda e, r=tt: e.dma_start(
                out=hqb[:], in_=hq_scr[r * 128:(r + 1) * 128, :]),
                wait=lhw, inc=("ldh", 16))
            lxw = [("ph", bar1)] if tt < 2 else [("vq", st2["x1", tt - 2])]
            t_lx = s.on("sync", lambda e, r=tt, b=x0b[pp]: e.dma_start(
                out=b[:], in_=x_in[r * 128:(r + 1) * 128, :]),
                wait=lxw, inc=("ldx%d" % (tt % 2), 16))
            evh = None
            k0 = 0
            first = True
            while k0 < KI:
                nk = min(8, KI - k0)
                w0 = [("ldh", t_lh), ("sq", evd[0])] if first else []
                if first and ("dn", tt - 1) in st2:
                    w0.append(("tq", st2["dn", tt - 1]))
                first = False
                evh = emit_trb(
                    [hqb[:, (k0 + k) * 128:(k0 + k + 1) * 128] for k in range(nk)],
                    hqTb[:, k0 * 128:(k0 + nk) * 128].rearrange(
                        "p (k m) -> p k m", m=128), w0)
                k0 += nk
            st2["hqT", tt] = evh

            def mk_dn(te):
                last = None
                for hi in range(2):
                    for k in range(KI):
                        last = te.matmul(
                            pbf[hi][:, 0:512],
                            hqTb[:, k * 128:(k + 1) * 128],
                            wdT_b[:, k * D + hi * 512:k * D + hi * 512 + 512],
                            start=(k == 0), stop=(k == KI - 1))
                return last
            wdn = [("sq", evh)]
            if ("x1", tt - 1) in st2:
                wdn.append(("vq", st2["x1", tt - 1]))
            t_dn = s.on("tensor", mk_dn, wait=wdn, inc="tq")
            st2["dn", tt] = t_dn

            def mk_x1(v, _pp=pp):
                v.drain()
                v.scalar_tensor_tensor(x1b[_pp][:, 0:512], pbf[0][:, 0:512],
                                       0.5, x0b[_pp][:, 0:512], ALU.mult, ALU.add)
                v.drain()
                return v.scalar_tensor_tensor(x1b[_pp][:, 512:1024],
                                              pbf[1][:, 0:512], 0.5,
                                              x0b[_pp][:, 512:1024],
                                              ALU.mult, ALU.add)
            x1w = [("tq", t_dn), ("ldx%d" % (tt % 2), t_lx)]
            if ("lb", tt - 2) in st2:
                x1w.append(("sq", st2["lb", tt - 2]))
            if ("sx1", tt - 2) in st2:
                x1w.append(("swx%d" % (tt % 2), st2["sx1", tt - 2]))
            t_x1 = s.on("vector", mk_x1, wait=x1w, inc="vq")
            st2["x1", tt] = t_x1
            t_sx1 = s.on("gpsimd", lambda e, r=tt, b=x1b[pp]: e.dma_start(
                out=x1_scr[r * 128:(r + 1) * 128, :], in_=b[:]),
                wait=[("vq", t_x1)], inc=("swx%d" % (tt % 2), 16))
            st2["sx1", tt] = t_sx1
            t_ln2, t_rs2 = ln_stats(
                x1b[pp], pp, [("vq", t_x1)],
                wr_free=([("sq", st2["lb", tt - 2])] if ("lb", tt - 2) in st2 else []),
                rs_free=([("sq", st2["lb", tt - 2])] if ("lb", tt - 2) in st2 else []))
            t_nm = s.on("vector", lambda v, _p=pp: (v.drain(), v.tensor_scalar(
                nmrb[_p][:], negmeanb[_p][:], rstdb[_p][:, 0:1], None, ALU.mult))[1],
                wait=[("gq", t_rs2)], inc="vq")
            if tt >= 1:
                attn_block(tt - 1)
            lbw = [("vq", t_nm)]
            if ("qe", tt - 1, NPAIR - 1) in st2:
                lbw.append(("sq", st2["qe", tt - 1, NPAIR - 1]))
            t_lb = s.on("scalar", lambda e, b=x1b[pp], _p=pp: e.activation(
                ln2bb[:], b[:], AF.Identity, scale=rstdb[_p][:, 0:1],
                bias=nmrb[_p][:, 0:1]), wait=lbw, inc="sq")
            st2["lb", tt] = t_lb
            evl = emit_trb([ln2bb[:, k * 128:(k + 1) * 128] for k in range(KD)],
                           ln2Tb[:].rearrange("p (k m) -> p k m", m=128),
                           [("sq", t_lb)])
            st2["evl", tt] = evl
        attn_block(NT - 1)

        last_qmax = st2["qmax", NT - 1]
        last_dup = s.val("std")
        t_kprev = None
        for j in range((T + 511) // 512):
            kw = min(512, T - j * 512)
            t_k2 = s.on("vector", lambda v, _j=j, _w=kw: (v.drain(),
                v.tensor_tensor(k2b[:, 0:_w], KTd[:, _j * 512:_j * 512 + _w],
                                KTd[:, _j * 512:_j * 512 + _w], ALU.mult))[1],
                wait=([("std", last_dup), ("vq", last_qmax)] if j == 0 else []),
                inc="vq")
            t_km = s.on("tensor", lambda te, _w=kw: te.matmul(
                pbf[4][0:2, 0:_w], onesblk[:, 0:2], k2b[:, 0:_w],
                start=True, stop=True),
                wait=[("vq", t_k2)] + ([("vq", t_kprev)] if t_kprev else []),
                inc="tq")
            t_kx = s.on("vector", lambda v, _w=kw: (v.drain(), v.reduce_max(
                cbuf[0:2, 1:2], pbf[4][0:2, 0:_w], axis=AX.X), v.drain(),
                v.tensor_tensor(cbuf[0:2, 2:3], cbuf[0:2, 2:3], cbuf[0:2, 1:2],
                                ALU.max))[-1],
                wait=[("tq", t_km)], inc="vq")
            t_kprev = t_kx
        t_m1 = s.on("sync", lambda e: e.dma_start(out=cbuf[0:1, 4:5],
                                                  in_=qmaxb[1:2, 0:1]),
                    wait=[("vq", t_kprev)], inc=("st", 16))
        t_m2 = s.on("sync", lambda e: e.dma_start(out=cbuf[0:1, 5:6],
                                                  in_=cbuf[1:2, 2:3]),
                    wait=[("st", t_m1)], inc=("st", 16))

        def v_c(v):
            v.drain()
            v.tensor_tensor(cbuf[0:1, 6:7], qmaxb[0:1, 0:1], cbuf[0:1, 4:5],
                            ALU.max)
            v.tensor_tensor(cbuf[0:1, 7:8], cbuf[0:1, 2:3], cbuf[0:1, 5:6],
                            ALU.max)
            v.drain()
            return v.tensor_tensor(cbuf[0:1, 8:9], cbuf[0:1, 6:7],
                                   cbuf[0:1, 7:8], ALU.mult)
        t_vc = s.on("vector", v_c, wait=[("st", t_m2)], inc="vq")
        t_pc = s.on("gpsimd", lambda g: (g.drain(), g.tensor_tensor(
            cbuf[0:1, 9:10], cbuf[0:1, 8:9], half[0:1, 0:1], ALU.pow))[1],
            wait=[("vq", t_vc)], inc="gq")
        TEMP_BISECT = True
        t_bc = s.on("tensor", lambda te: te.matmul(
            pbf[4][:, 0:1], onesc[0:1, 0:128], cbuf[0:1, 9:10],
            start=True, stop=True), wait=[("gq", t_pc)], inc="tq")
        t_negc = s.on("scalar", lambda e: e.activation(
            negc[:], pbf[4][:, 0:1], AF.Copy, scale=-1.0),
            wait=[("tq", t_bc)], inc="sq")
        s.on("scalar", lambda e: e.drain())

        p2 = {}
        bankfree = {}

        for cc in range(NCH):
            lxw = [("ph", bar1)]
            if ("x2v", cc - 2) in p2:
                lxw.append(("vq", p2["x2v", cc - 2]))
            t_lx1 = s.on("sync", lambda e, _c=cc: e.dma_start(
                out=x1cb[_c % 2][:].rearrange("p (t d) -> p t d", t=CHT),
                in_=x1_scr.rearrange("(t p) d -> p t d", p=128)[
                    :, _c * CHT:(_c + 1) * CHT, :]),
                wait=lxw + [("swx%d" % ((NT - 2) % 2), st2["sx1", NT - 2]),
                            ("swx%d" % ((NT - 1) % 2), st2["sx1", NT - 1]),
                            ("sww", t_spd)] if cc == 0 else lxw,
                inc=("ldx0", 16))
            lq_last = None
            for p0 in range(NPAIR):
                lqw = []
                if cc == 0 and p0 == 0:
                    lqw.append(("stq", st2["qs", NT - 1]))
                if p0 == 0 and ("simlast", cc - 1) in p2:
                    lqw.append(("tq", p2["simlast", cc - 1]))
                lq_last = s.on("sync", lambda e, _c=cc, _p=p0: e.dma_start(
                    out=qchb[:, _p * CH:(_p + 1) * CH],
                    in_=q_scr[_p, :, _c * CH:(_c + 1) * CH]),
                    wait=lqw, inc=("ldq", 16))
            for p0 in range(NPAIR):
                par = (cc + p0) % 2
                bp = 4 if par == 0 else 6
                pvA = pbf[bp]
                pvB = pbf[bp + 1]
                pvW = pbwf[bp // 2]

                def emit_sim(mt):
                    sb0 = (mt % 2) * 2

                    def mk_sim(te, _m=mt, _b=sb0, _p=p0):
                        te.matmul(pbwf[_b // 2][:, 0:CH],
                                  KTd[0:64, _m * 128:(_m + 1) * 128],
                                  qchb[0:64, _p * CH:(_p + 1) * CH],
                                  start=True, stop=True)
                        return te.matmul(pbwf[_b // 2][:, CH:2 * CH],
                                         KTd[64:128, _m * 128:(_m + 1) * 128],
                                         qchb[64:128, _p * CH:(_p + 1) * CH],
                                         start=True, stop=True)
                    w = []
                    if mt == 0 and p0 == 0:
                        w += [("sq", t_negc), ("ldq", lq_last)]
                    pe = p2.get(("exp", cc, p0, mt - 2))
                    if pe is None and mt < 2:
                        pe = p2.get(("exp", cc, p0 - 1, MT - 2 + mt)) or \
                            p2.get(("exp", cc - 1, NPAIR - 1, MT - 2 + mt))
                    if pe is not None:
                        w.append(("sq", pe))
                    t_sim = s.on("tensor", mk_sim, wait=w, inc="tq")
                    if mt == MT - 1:
                        p2["simlast", cc] = t_sim

                    def mk_exp(e, _b=sb0, _pp=(mt % 4)):
                        return e.activation(attb[_pp][:, 0:2 * CH],
                                            pbwf[_b // 2][:, 0:2 * CH],
                                            AF.Exp, bias=negc[:, 0:1])
                    we = [("tq", t_sim)]
                    pv = p2.get(("pv", cc, p0, mt - 4))
                    if pv is None and mt < 4:
                        mt2 = mt + 4 * ((MT - 1 - mt) // 4)
                        pv = p2.get(("pv", cc, p0 - 1, mt2)) or \
                            p2.get(("pv", cc - 1, NPAIR - 1, mt2))
                    if pv is not None:
                        we.append(("tq", pv))
                    t_exp = s.on("scalar", mk_exp, wait=we, inc="sq")
                    p2["exp", cc, p0, mt] = t_exp

                def emit_pv(mt):
                    def mk_pv(te, _m=mt, _pp=(mt % 4), _A=pvA, _B=pvB):
                        te.matmul(_A[0:65, 0:CH],
                                  Vp[:, _m * VSTR:_m * VSTR + 65],
                                  attb[_pp][:, 0:CH],
                                  start=(_m == 0), stop=(_m == MT - 1))
                        return te.matmul(_B[0:65, 0:CH],
                                         Vp[:, _m * VSTR:_m * VSTR + 65],
                                         attb[_pp][:, CH:2 * CH],
                                         start=(_m == 0), stop=(_m == MT - 1))
                    wp = [("sq", p2["exp", cc, p0, mt])]
                    if mt == 0 and par in bankfree:
                        wp += bankfree[par]
                    t_pv = s.on("tensor", mk_pv, wait=wp, inc="tq")
                    p2["pv", cc, p0, mt] = t_pv
                    return t_pv

                for mt in range(MT):
                    emit_sim(mt)
                    if mt >= 2:
                        emit_pv(mt - 2)
                if MT >= 2:
                    emit_pv(MT - 2)
                pv_last = emit_pv(MT - 1)

                ssw = [("tq", pv_last)]
                prev_s2 = p2.get(("s2", cc, p0 - 1)) or \
                    p2.get(("s2", cc - 1, NPAIR - 1))
                if prev_s2 is not None:
                    ssw.append(("st", prev_s2))
                t_sa = s.on("vector", lambda v, _w=pvW: (v.drain(), v.tensor_copy(
                    sumsb[64:65, 0:2 * CH].rearrange("p (k c) -> p k c", c=CH),
                    _w[64:65, :].rearrange("p (k c) -> p k c", k=2)[:, :, 0:CH]))[1],
                    wait=ssw, inc="vq")
                s1w = [("vq", t_sa)]
                prev_b2w = p2.get(("b2", cc, p0 - 1)) or \
                    p2.get(("b2", cc - 1, NPAIR - 1))
                if prev_b2w is not None:
                    s1w.append(("st", prev_b2w))
                t_s1 = s.on("sync", lambda e: e.dma_start(
                    out=rsum2[0:1, 0:CH], in_=sumsb[64:65, 0:CH]),
                    wait=s1w, inc=("st", 16))
                t_s2 = s.on("sync", lambda e: e.dma_start(
                    out=rsum2[1:2, 0:CH], in_=sumsb[64:65, CH:2 * CH]),
                    wait=[("st", t_s1)], inc=("st", 16))
                p2["s2", cc, p0] = t_s2
                t_rc = s.on("vector", lambda v: (v.drain(), v.reciprocal(
                    rsum2[0:2, 0:CH], rsum2[0:2, 0:CH]))[1],
                    wait=[("st", t_s2)], inc="vq")
                sdw = [("vq", t_rc)]
                prev_b2 = p2.get(("b2", cc, p0 - 1)) or \
                    p2.get(("b2", cc - 1, NPAIR - 1))
                if prev_b2 is not None:
                    sdw.append(("st", prev_b2))
                t_sd = s.on("sync", lambda e: e.dma_start(
                    out=rs_scr, in_=rsum2[0:2, 0:CH]), wait=sdw, inc=("st", 16))
                obw0 = []
                prev_ob = p2.get(("ob", cc, p0 - 1)) or \
                    p2.get(("ob", cc - 1, NPAIR - 1))
                if prev_ob is not None:
                    obw0.append(("vq", prev_ob))
                t_b1 = s.on("sync", lambda e: e.dma_start(
                    out=rsbb[0:64, 0:CH].unsqueeze(1),
                    in_=rs_scr[0:1, :].partition_broadcast(64)),
                    wait=[("st", t_sd)] + obw0, inc=("st", 16))
                t_b2 = s.on("sync", lambda e: e.dma_start(
                    out=rsbb[64:128, 0:CH].unsqueeze(1),
                    in_=rs_scr[1:2, :].partition_broadcast(64)),
                    wait=[("st", t_b1)], inc=("st", 16))
                p2["b2", cc, p0] = t_b2
                t_oa = s.on("vector", lambda v, _p=p0, _a=pvA: (v.drain(),
                            v.tensor_tensor(
                                ostk[0:64, _p * CH:(_p + 1) * CH],
                                _a[0:64, 0:CH], rsbb[0:64, 0:CH],
                                ALU.mult))[1],
                            wait=[("st", t_b2)], inc="vq")
                obw = []
                prev_oh = p2.get(("oh", cc, p0 - 1)) or \
                    p2.get(("oh", cc - 1, NPAIR - 1))
                if prev_oh is not None:
                    obw.append(("sto", prev_oh))
                t_ob = s.on("vector", lambda v, _b=pvB: (v.drain(), v.tensor_tensor(
                    oBb[0:64, 0:CH], _b[0:64, 0:CH], rsbb[64:128, 0:CH],
                    ALU.mult))[1], wait=obw, inc="vq")
                p2["ob", cc, p0] = t_ob
                t_oh = s.on("sync", lambda e, _p=p0: e.dma_start(
                    out=ostk[64:128, _p * CH:(_p + 1) * CH],
                    in_=oBb[0:64, 0:CH]),
                    wait=[("vq", t_ob)], inc=("sto", 16))
                p2["oh", cc, p0] = t_oh
                p2["oa", cc, p0] = t_oa
                bankfree[par] = [("vq", t_ob)]
            prj_par = (cc + 6) % 2
            prjA = pbf[4 if prj_par == 0 else 6]
            prjB = pbf[(4 if prj_par == 0 else 6) + 1]
            t_prev = None
            for si in range(CHT):
                def mk_prj(te, _s=si, _A=prjA, _B=prjB):
                    last = None
                    for p0 in range(NPAIR):
                        for hh in range(2):
                            last = te.matmul(
                                (_A if hh == 0 else _B)[:, 0:512],
                                ostk[:, p0 * CH + _s * 128:p0 * CH + (_s + 1) * 128],
                                woT[:, p0 * D + hh * 512:p0 * D + (hh + 1) * 512],
                                start=(p0 == 0), stop=(p0 == NPAIR - 1))
                    return last
                wpr = [("sto", p2["oh", cc, NPAIR - 1]),
                       ("vq", p2["oa", cc, NPAIR - 1]),
                       ("vq", p2["ob", cc, NPAIR - 1])]
                if si == 0 and prj_par in bankfree:
                    wpr += bankfree[prj_par]
                if t_prev is not None:
                    wpr.append(("vq", t_prev))
                t_pr = s.on("tensor", mk_prj, wait=wpr, inc="tq")

                def mk_x2(v, _s=si, _A=prjA, _B=prjB, _c=cc):
                    v.drain()
                    v.tensor_tensor(x2cb[_c % 2][:, _s * D:_s * D + 512],
                                    x1cb[_c % 2][:, _s * D:_s * D + 512],
                                    _A[:, 0:512], ALU.add)
                    v.drain()
                    return v.tensor_tensor(
                        x2cb[_c % 2][:, _s * D + 512:_s * D + 1024],
                        x1cb[_c % 2][:, _s * D + 512:_s * D + 1024],
                        _B[:, 0:512], ALU.add)
                xw = [("tq", t_pr), ("ldx0", t_lx1)]
                if si == 0 and ("x2s", cc - 2) in p2:
                    xw.append(("swc", p2["x2s", cc - 2]))
                t_x2 = s.on("vector", mk_x2, wait=xw, inc="vq")
                t_prev = t_x2
            p2["x2v", cc] = t_prev
            bankfree[prj_par] = [("vq", t_prev)]
            t_sx2 = s.on("gpsimd", lambda e, _c=cc: e.dma_start(
                out=x2_scr.rearrange("(t p) d -> p t d", p=128)[
                    :, _c * CHT:(_c + 1) * CHT, :],
                in_=x2cb[_c % 2][:].rearrange("p (t d) -> p t d", t=CHT)),
                wait=[("vq", t_prev)], inc=("swc", 16))
            p2["x2s", cc] = t_sx2

        bar2 = s.barrier(dma_sems=DMAS)

        t_rg = s.on("sync", lambda e: e.dma_start(out=wgT_c, in_=wgT_scr),
                    wait=[("ph", bar2)], inc=("ldw0", 16))
        t_ru = s.on("scalar", lambda e: e.dma_start(out=wuT_c, in_=wuT_scr),
                    wait=[("ph", bar2)], inc=("ldw0", 16))
        t_rd = s.on("gpsimd", lambda e: e.dma_start(out=wdT_c, in_=wdT_scr),
                    wait=[("ph", bar2)], inc=("sww", 16))
        t_rdw = s.val("sww")

        st3 = {}

        def loadx2(r, buf, extra):
            w = list(extra)
            if r == 0:
                w += [("ldw0", t_ru), ("sww", t_rdw),
                      ("swc", p2["x2s", NCH - 1])]
            if ("outv", r - 2) in st3:
                w.append(("vq", st3["outv", r - 2]))
            return s.on("sync", lambda e, _r=r, _b=buf: e.dma_start(
                out=_b[:], in_=x2_scr[_r * 128:(_r + 1) * 128, :]),
                wait=w, inc=("ldx%d" % (r % 2), 16))

        def p3_tr(tt):
            tokh = st3["hq", tt, NGC - 1]
            evh = None
            k0 = 0
            first = True
            while k0 < KI:
                nk = min(8, KI - k0)
                w0 = [("vq", tokh)] if first else []
                if first and ("dn", tt - 1) in st3:
                    w0.append(("tq", st3["dn", tt - 1]))
                first = False
                evh = emit_trb(
                    [hqb[:, (k0 + k) * 128:(k0 + k + 1) * 128] for k in range(nk)],
                    hqTb[:, k0 * 128:(k0 + nk) * 128].rearrange(
                        "p (k m) -> p k m", m=128), w0)
                k0 += nk
            st3["hqfree", tt] = ("sq", evh)
            st3["evh", tt] = evh

        def p3_down_part(tt, hi, k0, k1):
            def mk_dn(te, _hi=hi, _k0=k0, _k1=k1):
                last = None
                for k in range(_k0, _k1):
                    last = te.matmul(
                        pbf[4 + _hi][:, 0:512],
                        hqTb[:, k * 128:(k + 1) * 128],
                        wdT_c[:, k * D + _hi * 512:k * D + _hi * 512 + 512],
                        start=(k == 0), stop=(k == KI - 1))
                return last
            wdn = []
            if k0 == 0:
                wdn.append(("sq", st3["evh", tt]))
                if ("outv", tt - 1) in st3:
                    wdn.append(("vq", st3["outv", tt - 1]))
            t_dn = s.on("tensor", mk_dn, wait=wdn, inc="tq")
            if hi == 1 and k1 == KI:
                st3["dn", tt] = t_dn

        def p3_out(tt, pp):
            def mk_out(v, _pp=pp):
                v.drain()
                v.scalar_tensor_tensor(x1b[_pp][:, 0:512], pbf[4][:, 0:512],
                                       0.5, x0b[_pp][:, 0:512],
                                       ALU.mult, ALU.add)
                v.drain()
                return v.scalar_tensor_tensor(x1b[_pp][:, 512:1024],
                                              pbf[5][:, 0:512], 0.5,
                                              x0b[_pp][:, 512:1024],
                                              ALU.mult, ALU.add)
            ow = [("tq", st3["dn", tt])]
            if ("so", tt - 2) in st3:
                ow.append(("so%d" % (tt % 2), st3["so", tt - 2]))
            t_out = s.on("vector", mk_out, wait=ow, inc="vq")
            st3["outv", tt] = t_out
            t_so = s.on("gpsimd", lambda e, r=tt, b=x1b[pp]: e.dma_start(
                out=out_hbm[r * 128:(r + 1) * 128, :], in_=b[:]),
                wait=[("vq", t_out)], inc=("so%d" % (tt % 2), 16))
            st3["so", tt] = t_so

        KH = (KI + 1) // 2

        def hooks3(tt):
            if tt == 0:
                return None
            pt = tt - 1
            ppt = pt % 2
            return {0: (lambda _t=pt: p3_down_part(_t, 0, 0, KH)),
                    1: (lambda _t=pt: p3_down_part(_t, 0, KH, KI)),
                    2: (lambda _t=pt: p3_down_part(_t, 1, 0, KH)),
                    3: (lambda _t=pt, _p=ppt: (
                        p3_down_part(_t, 1, KH, KI), p3_out(_t, _p))[1])}

        def on_last3(tt_, tok):
            p3_tr(tt_)

        ffn_phase(loadx2, st3, wgT_c, wuT_c, hooks3, on_last3, tile_flush=True)
        p3_down_part(NT - 1, 0, 0, KI)
        p3_down_part(NT - 1, 1, 0, KI)
        p3_out(NT - 1, (NT - 1) % 2)

        s.on("sync", lambda e: e.nop(),
             wait=[("so0", s.val("so0")), ("so1", s.val("so1"))])
        s.play(block)
    return nc


def make_consts():
    ident = np.eye(P, dtype=ml_dtypes.bfloat16)
    onesblk = np.zeros((P, 2), dtype=np.float32)
    onesblk[0:64, 0] = 1.0
    onesblk[64:128, 1] = 1.0
    ones2 = np.zeros((2, P), dtype=np.float32)
    ones2[0, 0:64] = 1.0
    ones2[1, 64:128] = 1.0
    onesc = np.ones((1, P), dtype=np.float32)
    return {"identc": ident, "onesblkc": onesblk, "ones2c": ones2,
            "onescc": onesc}


def kernel(L0_attn_x, L0_attn_query_proj, L0_attn_key_proj, L0_attn_value_proj,
           L0_attn_output_proj, L0_ffn_w_gate, L0_ffn_w_up, L0_ffn_w_down):
    x = np.ascontiguousarray(np.asarray(L0_attn_x, dtype=np.float32))
    B, T, Dm = x.shape
    wq = np.ascontiguousarray(
        np.asarray(L0_attn_query_proj, dtype=np.float32).reshape(H * 64, Dm))
    wk = np.ascontiguousarray(np.asarray(L0_attn_key_proj, dtype=np.float32))
    wv = np.ascontiguousarray(np.asarray(L0_attn_value_proj, dtype=np.float32))
    wo = np.ascontiguousarray(
        np.asarray(L0_attn_output_proj, dtype=np.float32).reshape(Dm, H * 64))
    wg = np.ascontiguousarray(np.asarray(L0_ffn_w_gate, dtype=np.float32))
    wu = np.ascontiguousarray(np.asarray(L0_ffn_w_up, dtype=np.float32))
    wd = np.ascontiguousarray(np.asarray(L0_ffn_w_down, dtype=np.float32))
    consts = make_consts()
    nc = build_program(T=T)
    in_maps = []
    for c in range(B):
        m = {"x": np.ascontiguousarray(x[c]), "w_gate": wg, "w_up": wu,
             "w_down": wd, "q_proj": wq, "k_proj": wk, "v_proj": wv,
             "o_proj": wo}
        m.update(consts)
        in_maps.append(m)
    res = run_bass_kernel_spmd(nc, in_maps, core_ids=list(range(B)))
    out = np.stack([r["out"] for r in res.results], axis=0)
    return out.astype(np.float32)
